# version 02
# baseline (speedup 1.0000x reference)
import numpy as np
import ml_dtypes
import concourse.bass as bass
import concourse.mybir as mybir
from concourse.bass_utils import run_bass_kernel_spmd

F32 = mybir.dt.float32
BF16 = mybir.dt.bfloat16
I32 = mybir.dt.int32
AF = mybir.ActivationFunctionType
ALU = mybir.AluOpType
AX = mybir.AxisListType
BF = ml_dtypes.bfloat16

NCORE = 8
D = 4096
SEQ = 2048
B = 4
NTOK = B * SEQ
TPC = 1024
H1 = 11008
HC1 = H1 // 128
EPS = 1e-5
NEG = -1.0e9


class Ser:
    def __init__(self, nc, name="ser"):
        self.nc = nc
        self._cm = nc.semaphore(name)
        self.sem = self._cm.__enter__()
        self.v = 0

    def do(self, eng, fn, inc=1):
        eng.wait_ge(self.sem, self.v)
        inst = fn()
        inst.then_inc(self.sem, inc)
        self.v += inc
        return inst

    def chain(self, eng, fns, inc=1):
        eng.wait_ge(self.sem, self.v)
        for f in fns[:-1]:
            f()
        fns[-1]().then_inc(self.sem, inc)
        self.v += inc

    def fence(self, eng):
        eng.wait_ge(self.sem, self.v)


def make_identity(nc, S, ctx):
    iot = ctx(nc.sbuf_tensor("idn_iot", [128, 128], I32))
    iotc = ctx(nc.sbuf_tensor("idn_iotc", [128, 1], I32))
    iotf = ctx(nc.sbuf_tensor("idn_iotf", [128, 128], F32))
    iotcf = ctx(nc.sbuf_tensor("idn_iotcf", [128, 1], F32))
    idn = ctx(nc.sbuf_tensor("idn", [128, 128], BF16))
    S.do(nc.gpsimd, lambda: nc.gpsimd.iota(iot[:], pattern=[[1, 128]], channel_multiplier=0))
    S.do(nc.gpsimd, lambda: nc.gpsimd.iota(iotc[:], pattern=[[0, 1]], channel_multiplier=1))
    S.do(nc.vector, lambda: nc.vector.tensor_copy(iotf[:], iot[:]))
    S.do(nc.vector, lambda: nc.vector.tensor_copy(iotcf[:], iotc[:]))
    S.do(nc.vector, lambda: nc.vector.tensor_single_scalar(idn[:], iotf[:], iotcf[:], ALU.is_equal))
    return idn


def emit_ln_transpose(nc, S, x_dram, row0, ntile, xn_fm, col0, idn, d):
    xt = d["ln_xt"]
    xn_tm = d["ln_xn_tm"]
    s1 = d["ln_s1"]; ssq = d["ln_ssq"]; m = d["ln_m"]; var = d["ln_var"]
    t0 = d["ln_t0"]; rstd = d["ln_rstd"]; nmr = d["ln_nmr"]; pst = d["ln_pst"]
    FM_W = xn_fm.shape[1] // 32
    for t in range(ntile):
        S.do(nc.sync, lambda t=t: nc.sync.dma_start(
            xt[:], x_dram[row0 + 128 * t: row0 + 128 * (t + 1), :]), 16)
        S.do(nc.vector, lambda: nc.vector.reduce_sum(s1[:], xt[:], axis=AX.X))
        S.do(nc.scalar, lambda: nc.scalar.activation(xn_tm[:], xt[:], AF.Square, accum_out=ssq[:]))
        S.do(nc.vector, lambda: nc.vector.tensor_single_scalar(m[:], s1[:], 1.0 / D, ALU.mult))
        S.do(nc.vector, lambda: nc.vector.tensor_single_scalar(var[:], ssq[:], 1.0 / D, ALU.mult))
        S.do(nc.vector, lambda: nc.vector.tensor_mul(t0[:], m[:], m[:]))
        S.do(nc.vector, lambda: nc.vector.tensor_sub(var[:], var[:], t0[:]))
        S.do(nc.vector, lambda: nc.vector.tensor_single_scalar(var[:], var[:], EPS, ALU.add))
        S.do(nc.scalar, lambda: nc.scalar.activation(t0[:], var[:], AF.Ln))
        S.do(nc.scalar, lambda: nc.scalar.activation(rstd[:], t0[:], AF.Exp, scale=-0.5))
        S.do(nc.vector, lambda: nc.vector.tensor_mul(nmr[:], m[:], rstd[:]))
        S.do(nc.vector, lambda: nc.vector.tensor_single_scalar(nmr[:], nmr[:], -1.0, ALU.mult))
        S.do(nc.scalar, lambda: nc.scalar.activation(xn_tm[:], xt[:], AF.Identity,
                                                     bias=nmr[:], scale=rstd[:]))
        for dc in range(32):
            S.do(nc.tensor, lambda dc=dc: nc.tensor.transpose(
                pst[:], xn_tm[:, 128 * dc:128 * (dc + 1)], idn[:]))
            S.do(nc.vector, lambda dc=dc, t=t: nc.vector.tensor_copy(
                xn_fm[:, dc * FM_W + col0 + 128 * t: dc * FM_W + col0 + 128 * (t + 1)], pst[:]))


def alloc_ln_scratch(nc, ctx, tag=""):
    d = {}
    d["ln_xt"] = ctx(nc.sbuf_tensor("ln_xt" + tag, [128, 4096], F32))
    d["ln_xn_tm"] = ctx(nc.sbuf_tensor("ln_xn_tm" + tag, [128, 4096], BF16))
    for nm in ["ln_s1", "ln_ssq", "ln_m", "ln_var", "ln_t0", "ln_rstd", "ln_nmr"]:
        d[nm] = ctx(nc.sbuf_tensor(nm + tag, [128, 1], F32))
    d["ln_pst"] = ctx(nc.psum_tensor("ln_pst" + tag, [128, 128], BF16))
    return d


def emit_dequant(nc, S, st, pk1, s1, pk3, s3, pk2, s2, w1s, w3s, w2s):
    from contextlib import ExitStack
    with ExitStack() as stq:
        ctx = stq.enter_context
        pkt = ctx(nc.sbuf_tensor("dq_pkt", [128, 2048], I32))
        s_t = ctx(nc.sbuf_tensor("dq_st", [128, 64], F32))
        ms_i = ctx(nc.sbuf_tensor("dq_msi", [128, 2048], I32))
        ls_i = ctx(nc.sbuf_tensor("dq_lsi", [128, 2048], I32))
        ms_f = ctx(nc.sbuf_tensor("dq_msf", [128, 2048], F32))
        ls_f = ctx(nc.sbuf_tensor("dq_lsf", [128, 2048], F32))
        dqb = ctx(nc.sbuf_tensor("dq_dqb", [128, 4096], BF16))
        tb = ctx(nc.sbuf_tensor("dq_tb", [128, 32, 128], BF16))
        dq_e = dqb.ap().rearrange("p (g two v) -> p g two v", two=2, v=64)[:, :, 0, :]
        dq_o = dqb.ap().rearrange("p (g two v) -> p g two v", two=2, v=64)[:, :, 1, :]
        s_e = s_t.ap().rearrange("p (g two) -> p g two", two=2)[:, :, 0:1].to_broadcast((128, 32, 64))
        s_o = s_t.ap().rearrange("p (g two) -> p g two", two=2)[:, :, 1:2].to_broadcast((128, 32, 64))
        msv = ms_f.ap().rearrange("p (g v) -> p g v", v=64)
        lsv = ls_f.ap().rearrange("p (g v) -> p g v", v=64)
        for t in range(HC1):
            for pk, ss, stash in ((pk1, s1, w1s), (pk3, s3, w3s)):
                S.do(nc.sync, lambda pk=pk, t=t: nc.sync.dma_start(pkt[:], pk[t]), 16)
                S.do(nc.sync, lambda ss=ss, t=t: nc.sync.dma_start(s_t[:], ss[t]), 16)
                S.do(nc.vector, lambda: nc.vector.tensor_single_scalar(ms_i[:], pkt[:], 4, ALU.arith_shift_right))
                S.do(nc.vector, lambda: nc.vector.tensor_scalar(ls_i[:], pkt[:], 15, 8, op0=ALU.bitwise_and, op1=ALU.bitwise_xor))
                S.do(nc.vector, lambda: nc.vector.tensor_single_scalar(ls_i[:], ls_i[:], 8, ALU.subtract))
                S.do(nc.vector, lambda: nc.vector.tensor_copy(ms_f[:], ms_i[:]))
                S.do(nc.vector, lambda: nc.vector.tensor_copy(ls_f[:], ls_i[:]))
                S.do(nc.vector, lambda: nc.vector.tensor_tensor(out=dq_e, in0=msv, in1=s_e, op=ALU.mult))
                S.do(nc.vector, lambda: nc.vector.tensor_tensor(out=dq_o, in0=lsv, in1=s_o, op=ALU.mult))
                S.do(nc.sync, lambda: nc.sync.dma_start_transpose(tb[:], dqb[:]), 16)
                S.do(nc.sync, lambda stash=stash, t=t: nc.sync.dma_start(
                    stash[t], tb.ap().rearrange("p a b -> p (a b)")), 16)
    with ExitStack() as stq:
        ctx = stq.enter_context
        pkt2 = ctx(nc.sbuf_tensor("dq_pkt2", [128, 5504], I32))
        s2t = ctx(nc.sbuf_tensor("dq_s2t", [128, 172], F32))
        m2i = ctx(nc.sbuf_tensor("dq_m2i", [128, 5504], I32))
        m2f = ctx(nc.sbuf_tensor("dq_m2f", [128, 5504], F32))
        l2f = ctx(nc.sbuf_tensor("dq_l2f", [128, 5504], F32))
        dq2 = ctx(nc.sbuf_tensor("dq_dq2", [128, 11008], BF16))
        tb2 = ctx(nc.sbuf_tensor("dq_tb2", [128, 86, 128], BF16))
        d2_e = dq2.ap().rearrange("p (g two v) -> p g two v", two=2, v=64)[:, :, 0, :]
        d2_o = dq2.ap().rearrange("p (g two v) -> p g two v", two=2, v=64)[:, :, 1, :]
        s2_e = s2t.ap().rearrange("p (g two) -> p g two", two=2)[:, :, 0:1].to_broadcast((128, 86, 64))
        s2_o = s2t.ap().rearrange("p (g two) -> p g two", two=2)[:, :, 1:2].to_broadcast((128, 86, 64))
        m2v = m2f.ap().rearrange("p (g v) -> p g v", v=64)
        l2v = l2f.ap().rearrange("p (g v) -> p g v", v=64)
        w2v = w2s.rearrange("(hc p) (rc d) -> p hc rc d", p=128, d=128)
        for rc in range(32):
            S.do(nc.sync, lambda rc=rc: nc.sync.dma_start(pkt2[:], pk2[rc]), 16)
            S.do(nc.sync, lambda rc=rc: nc.sync.dma_start(s2t[:], s2[rc]), 16)
            S.do(nc.vector, lambda: nc.vector.tensor_single_scalar(m2i[:], pkt2[:], 4, ALU.arith_shift_right))
            S.do(nc.vector, lambda: nc.vector.tensor_copy(m2f[:], m2i[:]))
            S.do(nc.vector, lambda: nc.vector.tensor_scalar(m2i[:], pkt2[:], 15, 8, op0=ALU.bitwise_and, op1=ALU.bitwise_xor))
            S.do(nc.vector, lambda: nc.vector.tensor_single_scalar(m2i[:], m2i[:], 8, ALU.subtract))
            S.do(nc.vector, lambda: nc.vector.tensor_copy(l2f[:], m2i[:]))
            S.do(nc.vector, lambda: nc.vector.tensor_tensor(out=d2_e, in0=m2v, in1=s2_e, op=ALU.mult))
            S.do(nc.vector, lambda: nc.vector.tensor_tensor(out=d2_o, in0=l2v, in1=s2_o, op=ALU.mult))
            S.do(nc.sync, lambda: nc.sync.dma_start_transpose(tb2[:], dq2[:]), 16)
            S.do(nc.sync, lambda rc=rc: nc.sync.dma_start(w2v[:, :, rc, :], tb2[:]), 16)


def emit_ffn0(nc, S, idn, x, xout, row0, W, tag):
    from contextlib import ExitStack
    bwT, gwT, aaT, bal, erep = W
    with ExitStack() as st:
        ctx = st.enter_context
        xn = ctx(nc.sbuf_tensor("f0_xn" + tag, [128, 32 * 1024], BF16))
        with ExitStack() as stA:
            lns = alloc_ln_scratch(nc, stA.enter_context, "f0" + tag)
            emit_ln_transpose(nc, S, x, row0, 8, xn, 0, idn, lns)
        gwf = ctx(nc.sbuf_tensor("f0_gwf" + tag, [128, 28], F32))
        gwb = ctx(nc.sbuf_tensor("f0_gwb" + tag, [128, 32 * 28], BF16))
        lg_fm = ctx(nc.sbuf_tensor("f0_lgfm" + tag, [28, 1024], BF16))
        lg_tm = ctx(nc.sbuf_tensor("f0_lgtm" + tag, [128, 8 * 28], F32))
        g_fm = ctx(nc.sbuf_tensor("f0_gfm" + tag, [28, 1024], BF16))
        grep = ctx(nc.sbuf_tensor("f0_grep" + tag, [128, 2 * 1024], F32))
        h_fm = ctx(nc.sbuf_tensor("f0_hfm" + tag, [128, 2 * 1024], BF16))
        stBC = st.enter_context(ExitStack())
        ctxB = stBC.enter_context
        ps_g = ctxB(nc.psum_tensor("f0_psg" + tag, [28, 512], F32))
        ps_l1 = ctxB(nc.psum_tensor("f0_psl1" + tag, [128, 28], BF16))
        ps_l2 = ctxB(nc.psum_tensor("f0_psl2" + tag, [28, 128], BF16))
        ps_r = ctxB(nc.psum_tensor("f0_psr" + tag, [128, 512], F32))
        for dc in range(32):
            S.do(nc.sync, lambda dc=dc: nc.sync.dma_start(
                gwf[:], gwT[128 * dc:128 * (dc + 1), :]), 16)
            S.do(nc.scalar, lambda dc=dc: nc.scalar.copy(gwb[:, 28 * dc:28 * (dc + 1)], gwf[:]))
        for th in range(2):
            S.chain(nc.tensor, [
                (lambda dc=dc, th=th: nc.tensor.matmul(
                    ps_g[:], gwb[:, 28 * dc:28 * (dc + 1)],
                    xn[:, dc * 1024 + th * 512: dc * 1024 + th * 512 + 512],
                    start=(dc == 0), stop=(dc == 31)))
                for dc in range(32)])
            S.do(nc.vector, lambda th=th: nc.vector.tensor_copy(
                lg_fm[:, th * 512:(th + 1) * 512], ps_g[:]))
        for tt in range(8):
            S.do(nc.tensor, lambda tt=tt: nc.tensor.transpose(
                ps_l1[:], lg_fm[:, 128 * tt:128 * (tt + 1)], idn[0:28, 0:28]))
            S.do(nc.vector, lambda tt=tt: nc.vector.tensor_copy(
                lg_tm[:, 28 * tt:28 * (tt + 1)], ps_l1[:]))
        top8 = ctxB(nc.sbuf_tensor("f0_top8" + tag, [128, 8], F32))
        nm1 = ctxB(nc.sbuf_tensor("f0_nm1" + tag, [128, 1], F32))
        em2 = ctxB(nc.sbuf_tensor("f0_em2" + tag, [128, 1], F32))
        rec2 = ctxB(nc.sbuf_tensor("f0_rec2" + tag, [128, 1], F32))
        e_t = ctxB(nc.sbuf_tensor("f0_et" + tag, [128, 28], F32))
        msk = ctxB(nc.sbuf_tensor("f0_msk" + tag, [128, 28], F32))
        g_tm = ctxB(nc.sbuf_tensor("f0_gtm" + tag, [128, 8 * 28], BF16))
        for tt in range(8):
            lg = lg_tm[:, 28 * tt:28 * (tt + 1)]
            S.do(nc.vector, lambda lg=lg: nc.vector.max(out=top8[:], in_=lg))
            S.do(nc.vector, lambda: nc.vector.tensor_single_scalar(nm1[:], top8[:, 0:1], -1.0, ALU.mult))
            S.do(nc.vector, lambda: nc.vector.tensor_sub(em2[:], top8[:, 1:2], top8[:, 0:1]))
            S.do(nc.scalar, lambda: nc.scalar.activation(em2[:], em2[:], AF.Exp))
            S.do(nc.vector, lambda: nc.vector.tensor_single_scalar(em2[:], em2[:], 1.0, ALU.add))
            S.do(nc.vector, lambda: nc.vector.reciprocal(rec2[:], em2[:]))
            S.do(nc.vector, lambda: nc.vector.tensor_single_scalar(rec2[:], rec2[:], 2.0, ALU.mult))
            S.do(nc.scalar, lambda lg=lg: nc.scalar.activation(e_t[:], lg, AF.Exp, bias=nm1[:]))
            S.do(nc.vector, lambda lg=lg: nc.vector.tensor_single_scalar(msk[:], lg, top8[:, 1:2], ALU.is_ge))
            S.do(nc.vector, lambda: nc.vector.tensor_mul(e_t[:], e_t[:], msk[:]))
            S.do(nc.vector, lambda tt=tt: nc.vector.tensor_single_scalar(
                g_tm[:, 28 * tt:28 * (tt + 1)], e_t[:], rec2[:], ALU.mult))
        for tt in range(8):
            S.do(nc.tensor, lambda tt=tt: nc.tensor.transpose(
                ps_l2[:], g_tm[:, 28 * tt:28 * (tt + 1)], idn[:]))
            S.do(nc.vector, lambda tt=tt: nc.vector.tensor_copy(
                g_fm[:, 128 * tt:128 * (tt + 1)], ps_l2[:]))
        erf = ctxB(nc.sbuf_tensor("f0_erf" + tag, [28, 224], F32))
        erb = ctxB(nc.sbuf_tensor("f0_erb" + tag, [28, 224], BF16))
        S.do(nc.sync, lambda: nc.sync.dma_start(erf[:], erep), 16)
        S.do(nc.scalar, lambda: nc.scalar.copy(erb[:], erf[:]))
        for mc, mw in ((0, 128), (1, 96)):
            for th in range(2):
                S.chain(nc.tensor, [lambda mc=mc, mw=mw, th=th: nc.tensor.matmul(
                    ps_r[0:mw, :], erb[:, 128 * mc:128 * mc + mw],
                    g_fm[:, th * 512:(th + 1) * 512], start=True, stop=True)])
                S.do(nc.vector, lambda mc=mc, mw=mw, th=th: nc.vector.tensor_copy(
                    grep[0:mw, mc * 1024 + th * 512: mc * 1024 + (th + 1) * 512], ps_r[0:mw, :]))
        aaf = ctxB(nc.sbuf_tensor("f0_aaf" + tag, [128, 224], F32))
        aab = ctxB(nc.sbuf_tensor("f0_aab" + tag, [128, 32 * 224], BF16))
        for dc in range(32):
            S.do(nc.sync, lambda dc=dc: nc.sync.dma_start(
                aaf[:], aaT[128 * dc:128 * (dc + 1), :]), 16)
            S.do(nc.scalar, lambda dc=dc: nc.scalar.copy(aab[:, 224 * dc:224 * (dc + 1)], aaf[:]))
        for mc, mw in ((0, 128), (1, 96)):
            for th in range(2):
                S.chain(nc.tensor, [
                    (lambda dc=dc, mc=mc, mw=mw, th=th: nc.tensor.matmul(
                        ps_r[0:mw, :], aab[:, 224 * dc + 128 * mc: 224 * dc + 128 * mc + mw],
                        xn[:, dc * 1024 + th * 512: dc * 1024 + th * 512 + 512],
                        start=(dc == 0), stop=(dc == 31)))
                    for dc in range(32)])
                S.do(nc.vector, lambda mc=mc, mw=mw, th=th: nc.vector.tensor_mul(
                    h_fm[0:mw, mc * 1024 + th * 512: mc * 1024 + (th + 1) * 512],
                    ps_r[0:mw, :], grep[0:mw, mc * 1024 + th * 512: mc * 1024 + (th + 1) * 512]))
        stBC.close()
        wf = ctx(nc.sbuf_tensor("f0_wf" + tag, [128, 512], F32))
        wb = ctx(nc.sbuf_tensor("f0_wb" + tag, [128, 512], BF16))
        bbf = ctx(nc.sbuf_tensor("f0_bbf" + tag, [128, 512], F32))
        bb0 = ctx(nc.sbuf_tensor("f0_bb0" + tag, [128, 512], BF16))
        bb1 = ctx(nc.sbuf_tensor("f0_bb1" + tag, [96, 512], BF16))
        xr = ctx(nc.sbuf_tensor("f0_xr" + tag, [128, 512], F32))
        o_sb = ctx(nc.sbuf_tensor("f0_osb" + tag, [128, 512], F32))
        pso = [ctx(nc.psum_tensor(f"f0_pso{i}" + tag, [128, 512], F32)) for i in range(8)]
        for ob in range(8):
            S.do(nc.sync, lambda ob=ob: nc.sync.dma_start(bbf[:], bal[0:128, 512 * ob:512 * (ob + 1)]), 16)
            S.do(nc.scalar, lambda: nc.scalar.copy(bb0[:], bbf[:]))
            S.do(nc.sync, lambda ob=ob: nc.sync.dma_start(bbf[0:96, :], bal[128:224, 512 * ob:512 * (ob + 1)]), 16)
            S.do(nc.scalar, lambda: nc.scalar.copy(bb1[:], bbf[0:96, :]))
            for dc in range(32):
                S.do(nc.sync, lambda dc=dc, ob=ob: nc.sync.dma_start(
                    wf[:], bwT[128 * dc:128 * (dc + 1), 512 * ob:512 * (ob + 1)]), 16)
                S.do(nc.scalar, lambda: nc.scalar.copy(wb[:], wf[:]))
                S.chain(nc.tensor, [
                    (lambda tt=tt, dc=dc: nc.tensor.matmul(
                        pso[tt][:], xn[:, dc * 1024 + 128 * tt: dc * 1024 + 128 * (tt + 1)],
                        wb[:], start=(dc == 0), stop=False, skip_group_check=True))
                    for tt in range(8)])
            S.chain(nc.tensor, [
                (lambda tt=tt: nc.tensor.matmul(
                    pso[tt][:], h_fm[:, 128 * tt:128 * (tt + 1)], bb0[:],
                    start=False, stop=False, skip_group_check=True))
                for tt in range(8)])
            S.chain(nc.tensor, [
                (lambda tt=tt: nc.tensor.matmul(
                    pso[tt][:], h_fm[0:96, 1024 + 128 * tt: 1024 + 128 * (tt + 1)], bb1[:],
                    start=False, stop=True, skip_group_check=True))
                for tt in range(8)])
            for tt in range(8):
                S.do(nc.sync, lambda tt=tt, ob=ob: nc.sync.dma_start(
                    xr[:], x[row0 + 128 * tt: row0 + 128 * (tt + 1), 512 * ob:512 * (ob + 1)]), 16)
                S.do(nc.vector, lambda tt=tt: nc.vector.scalar_tensor_tensor(
                    o_sb[:], pso[tt][:], 0.5, xr[:], ALU.mult, ALU.add))
                S.do(nc.sync, lambda tt=tt, ob=ob: nc.sync.dma_start(
                    xout[row0 + 128 * tt: row0 + 128 * (tt + 1), 512 * ob:512 * (ob + 1)], o_sb[:]), 16)


def emit_ffn1(nc, S, idn, x, xout, row0, w1s, w3s, w2s, tag):
    from contextlib import ExitStack
    with ExitStack() as st:
        ctx = st.enter_context
        xn = ctx(nc.sbuf_tensor("f1_xn" + tag, [128, 32 * 1024], BF16))
        with ExitStack() as stA:
            lns = alloc_ln_scratch(nc, stA.enter_context, "f1" + tag)
            emit_ln_transpose(nc, S, x, row0, 8, xn, 0, idn, lns)
        g_buf = ctx(nc.sbuf_tensor("f1_gbuf" + tag, [128, HC1 * 512], BF16))
        w1sb = ctx(nc.sbuf_tensor("f1_w1sb" + tag, [128, 4096], BF16))
        w3sb = ctx(nc.sbuf_tensor("f1_w3sb" + tag, [128, 4096], BF16))
        a_s = ctx(nc.sbuf_tensor("f1_as" + tag, [128, 512], BF16))
        b_s = ctx(nc.sbuf_tensor("f1_bs" + tag, [128, 512], BF16))
        w2sb = ctx(nc.sbuf_tensor("f1_w2sb" + tag, [128, 512], BF16))
        xr = ctx(nc.sbuf_tensor("f1_xr" + tag, [128, 512], F32))
        o_sb = ctx(nc.sbuf_tensor("f1_osb" + tag, [128, 512], F32))
        psA = ctx(nc.psum_tensor("f1_psA" + tag, [128, 512], F32))
        psB = ctx(nc.psum_tensor("f1_psB" + tag, [128, 512], F32))
        pso = [ctx(nc.psum_tensor(f"f1_pso{i}" + tag, [128, 512], F32)) for i in range(4)]
        for th in range(2):
            for hc in range(HC1):
                S.do(nc.sync, lambda hc=hc: nc.sync.dma_start(w1sb[:], w1s[hc]), 16)
                S.do(nc.sync, lambda hc=hc: nc.sync.dma_start(w3sb[:], w3s[hc]), 16)
                S.chain(nc.tensor, [
                    (lambda dc=dc, th=th: nc.tensor.matmul(
                        psA[:], w1sb[:, 128 * dc:128 * (dc + 1)],
                        xn[:, dc * 1024 + th * 512: dc * 1024 + th * 512 + 512],
                        start=(dc == 0), stop=(dc == 31)))
                    for dc in range(32)])
                S.chain(nc.tensor, [
                    (lambda dc=dc, th=th: nc.tensor.matmul(
                        psB[:], w3sb[:, 128 * dc:128 * (dc + 1)],
                        xn[:, dc * 1024 + th * 512: dc * 1024 + th * 512 + 512],
                        start=(dc == 0), stop=(dc == 31)))
                    for dc in range(32)])
                S.do(nc.scalar, lambda: nc.scalar.activation(a_s[:], psA[:], AF.Silu))
                S.do(nc.vector, lambda: nc.vector.tensor_copy(b_s[:], psB[:]))
                S.do(nc.vector, lambda hc=hc: nc.vector.tensor_mul(
                    g_buf[:, hc * 512:(hc + 1) * 512], a_s[:], b_s[:]))
            for ob in range(8):
                for hc in range(HC1):
                    S.do(nc.sync, lambda hc=hc, ob=ob: nc.sync.dma_start(
                        w2sb[:], w2s[128 * hc:128 * (hc + 1), 512 * ob:512 * (ob + 1)]), 16)
                    S.chain(nc.tensor, [
                        (lambda tt=tt, hc=hc: nc.tensor.matmul(
                            pso[tt][:], g_buf[:, hc * 512 + 128 * tt: hc * 512 + 128 * (tt + 1)],
                            w2sb[:], start=(hc == 0), stop=(hc == HC1 - 1), skip_group_check=True))
                        for tt in range(4)])
                for tt in range(4):
                    S.do(nc.sync, lambda tt=tt, th=th, ob=ob: nc.sync.dma_start(
                        xr[:], x[row0 + th * 512 + 128 * tt: row0 + th * 512 + 128 * (tt + 1),
                                 512 * ob:512 * (ob + 1)]), 16)
                    S.do(nc.vector, lambda tt=tt: nc.vector.scalar_tensor_tensor(
                        o_sb[:], pso[tt][:], 0.5, xr[:], ALU.mult, ALU.add))
                    S.do(nc.sync, lambda tt=tt, th=th, ob=ob: nc.sync.dma_start(
                        xout[row0 + th * 512 + 128 * tt: row0 + th * 512 + 128 * (tt + 1),
                             512 * ob:512 * (ob + 1)], o_sb[:]), 16)


def emit_attn(nc, S, idn, cfg):
    from contextlib import ExitStack
    x = cfg["x"]; dst = cfg["dst"]; nq = cfg["nq"]
    wqT = cfg["wqT"]; wkT = cfg["wkT"]; wvT = cfg["wvT"]; woT = cfg["woT"]
    bqkv = cfg["bqkv"]; bo = cfg["bo"]
    cos = cfg["cos"]; sin = cfg["sin"]; maskT = cfg["maskT"]
    scale = cfg["scale"]; q_hbm = cfg["q_hbm"]; ctx_hbm = cfg["ctx_hbm"]
    tag = cfg["tag"]
    nqh = nq // 512
    k_fm = None; v_tm = None
    with ExitStack() as st:
        ctx = st.enter_context
        k_fm = ctx(nc.sbuf_tensor("at_kfm" + tag, [128, 8 * 2048], BF16))
        v_tm = ctx(nc.sbuf_tensor("at_vtm" + tag, [128, 8 * 2048], BF16))
        with ExitStack() as stA:
            ctxA = stA.enter_context
            lns = alloc_ln_scratch(nc, ctxA, "at" + tag)
            xnb = ctxA(nc.sbuf_tensor("at_xnb" + tag, [128, 32 * 512], BF16))
            cos_s = ctxA(nc.sbuf_tensor("at_cos" + tag, [128, 512], F32))
            sin_s = ctxA(nc.sbuf_tensor("at_sin" + tag, [128, 512], F32))
            bsb = ctxA(nc.sbuf_tensor("at_bsb" + tag, [128, 48], F32))
            wf4 = ctxA(nc.sbuf_tensor("at_wf4" + tag, [128, 512], F32))
            wb4 = ctxA(nc.sbuf_tensor("at_wb4" + tag, [128, 512], BF16))
            kvf = ctxA(nc.sbuf_tensor("at_kvf" + tag, [128, 512], F32))
            rot = ctxA(nc.sbuf_tensor("at_rot" + tag, [128, 512], F32))
            vb = ctxA(nc.sbuf_tensor("at_vb" + tag, [128, 512], BF16))
            pk4 = [ctxA(nc.psum_tensor(f"at_pk{i}" + tag, [128, 512], F32)) for i in range(4)]
            pvt = ctxA(nc.psum_tensor("at_pvt" + tag, [128, 128], BF16))
            S.do(nc.sync, lambda: nc.sync.dma_start(bsb[:], bqkv), 16)

            def rope_emit(dst_bf):
                S.do(nc.vector, lambda: nc.vector.tensor_single_scalar(rot[0:64, :], kvf[64:128, :], -1.0, ALU.mult))
                S.do(nc.vector, lambda: nc.vector.tensor_copy(rot[64:128, :], kvf[0:64, :]))
                S.do(nc.vector, lambda: nc.vector.tensor_mul(kvf[:], kvf[:], cos_s[:]))
                S.do(nc.vector, lambda: nc.vector.tensor_mul(rot[:], rot[:], sin_s[:]))
                S.do(nc.vector, lambda: nc.vector.tensor_add(dst_bf, kvf[:], rot[:]))

            for tg in range(4):
                emit_ln_transpose(nc, S, x, tg * 512, 4, xnb, 0, idn, lns)
                S.do(nc.sync, lambda tg=tg: nc.sync.dma_start(cos_s[:], cos[:, tg * 512:(tg + 1) * 512]), 16)
                S.do(nc.sync, lambda tg=tg: nc.sync.dma_start(sin_s[:], sin[:, tg * 512:(tg + 1) * 512]), 16)

                def round4(w_dram, r):
                    for dc in range(32):
                        S.do(nc.sync, lambda dc=dc, r=r, w_dram=w_dram: nc.sync.dma_start(
                            wf4[:], w_dram[128 * dc:128 * (dc + 1), 512 * r:512 * (r + 1)]), 16)
                        S.do(nc.scalar, lambda: nc.scalar.copy(wb4[:], wf4[:]))
                        S.chain(nc.tensor, [
                            (lambda i=i, dc=dc: nc.tensor.matmul(
                                pk4[i][:], wb4[:, 128 * i:128 * (i + 1)],
                                xnb[:, dc * 512:(dc + 1) * 512],
                                start=(dc == 0), stop=(dc == 31), skip_group_check=True))
                            for i in range(4)])

                for r in range(2):
                    round4(wkT, r)
                    for i in range(4):
                        g = 4 * r + i
                        S.do(nc.scalar, lambda i=i, g=g: nc.scalar.activation(
                            kvf[:], pk4[i][:], AF.Identity, bias=bsb[:, 32 + g:33 + g]))
                        rope_emit(k_fm[:, g * 2048 + tg * 512: g * 2048 + (tg + 1) * 512])
                for r in range(2):
                    round4(wvT, r)
                    for i in range(4):
                        g = 4 * r + i
                        S.do(nc.scalar, lambda i=i, g=g: nc.scalar.activation(
                            vb[:], pk4[i][:], AF.Identity, bias=bsb[:, 40 + g:41 + g]))
                        for j in range(4):
                            S.do(nc.tensor, lambda j=j: nc.tensor.transpose(
                                pvt[:], vb[:, 128 * j:128 * (j + 1)], idn[:]))
                            S.do(nc.vector, lambda j=j, g=g, tg=tg: nc.vector.tensor_copy(
                                v_tm[:, (g * 16 + tg * 4 + j) * 128:(g * 16 + tg * 4 + j) * 128 + 128],
                                pvt[:]))
                if tg * 512 < nq:
                    for r in range(8):
                        round4(wqT, r)
                        for i in range(4):
                            h = 4 * r + i
                            S.do(nc.scalar, lambda i=i, h=h: nc.scalar.activation(
                                kvf[:], pk4[i][:], AF.Identity, bias=bsb[:, h:h + 1]))
                            rope_emit(vb[:])
                            S.do(nc.sync, lambda h=h, tg=tg: nc.sync.dma_start(
                                q_hbm[128 * h:128 * (h + 1), tg * 512:(tg + 1) * 512], vb[:]), 16)
        with ExitStack() as stB:
            ctxB = stB.enter_context
            mkq = ctxB(nc.sbuf_tensor("at_mkq" + tag, [128, 16 * 512], F32))
            ones_c = ctxB(nc.sbuf_tensor("at_onc" + tag, [128, 1], BF16))
            ones_r = ctxB(nc.sbuf_tensor("at_onr" + tag, [1, 128], F32))
            qh_sb = ctxB(nc.sbuf_tensor("at_qh" + tag, [128, 512], BF16))
            p_T = ctxB(nc.sbuf_tensor("at_pT" + tag, [128, 512], BF16))
            rec_d = ctxB(nc.sbuf_tensor("at_rec" + tag, [1, 512], F32))
            bc_sb = ctxB(nc.sbuf_tensor("at_bc" + tag, [128, 512], F32))
            ctx_sb = ctxB(nc.sbuf_tensor("at_ctx" + tag, [128, 512], BF16))
            ps_s = ctxB(nc.psum_tensor("at_pss" + tag, [128, 512], F32))
            ps_c = ctxB(nc.psum_tensor("at_psc" + tag, [128, 512], F32))
            ps_d = ctxB(nc.psum_tensor("at_psd" + tag, [1, 512], F32))
            ps_b = ctxB(nc.psum_tensor("at_psb" + tag, [128, 512], F32))
            S.do(nc.vector, lambda: nc.vector.memset(ones_c[:], 1.0))
            S.do(nc.vector, lambda: nc.vector.memset(ones_r[:], 1.0))
            for qh in range(nqh):
                for kt in range(16):
                    S.do(nc.sync, lambda kt=kt, qh=qh: nc.sync.dma_start(
                        mkq[:, kt * 512:(kt + 1) * 512],
                        maskT[128 * kt:128 * (kt + 1), qh * 512:(qh + 1) * 512]), 16)
                for h in range(32):
                    g = h // 4
                    S.do(nc.sync, lambda h=h, qh=qh: nc.sync.dma_start(
                        qh_sb[:], q_hbm[128 * h:128 * (h + 1), qh * 512:(qh + 1) * 512]), 16)
                    for kt in range(16):
                        S.chain(nc.tensor, [lambda g=g, kt=kt: nc.tensor.matmul(
                            ps_s[:], k_fm[:, g * 2048 + kt * 128: g * 2048 + kt * 128 + 128],
                            qh_sb[:], start=True, stop=True, skip_group_check=True)])
                        S.do(nc.vector, lambda kt=kt: nc.vector.tensor_add(
                            ps_s[:], ps_s[:], mkq[:, kt * 512:(kt + 1) * 512]))
                        S.do(nc.scalar, lambda: nc.scalar.activation(p_T[:], ps_s[:], AF.Exp, scale=scale))
                        S.chain(nc.tensor, [lambda g=g, kt=kt: nc.tensor.matmul(
                            ps_c[:], v_tm[:, (g * 16 + kt) * 128:(g * 16 + kt) * 128 + 128], p_T[:],
                            start=(kt == 0), stop=(kt == 15), skip_group_check=True)])
                        S.chain(nc.tensor, [lambda kt=kt: nc.tensor.matmul(
                            ps_d[:], ones_c[:], p_T[:],
                            start=(kt == 0), stop=(kt == 15), skip_group_check=True)])
                    S.do(nc.vector, lambda: nc.vector.reciprocal(rec_d[:], ps_d[:]))
                    S.chain(nc.tensor, [lambda: nc.tensor.matmul(
                        ps_b[:], ones_r[:], rec_d[:], start=True, stop=True, skip_group_check=True)])
                    S.do(nc.vector, lambda: nc.vector.tensor_copy(bc_sb[:], ps_b[:]))
                    S.do(nc.vector, lambda: nc.vector.tensor_mul(ctx_sb[:], ps_c[:], bc_sb[:]))
                    S.do(nc.sync, lambda h=h, qh=qh: nc.sync.dma_start(
                        ctx_hbm[128 * h:128 * (h + 1), qh * 512:(qh + 1) * 512], ctx_sb[:]), 16)
    from contextlib import ExitStack as ES2
    with ES2() as stC:
        ctxC = stC.enter_context
        cfm = ctxC(nc.sbuf_tensor("at_cfm" + tag, [128, 32 * 1024], BF16))
        bo_sb = ctxC(nc.sbuf_tensor("at_bo" + tag, [1, 4096], F32))
        bias_bc = ctxC(nc.sbuf_tensor("at_bbc" + tag, [128, 4096], F32))
        ones_r2 = ctxC(nc.sbuf_tensor("at_onr2" + tag, [1, 128], F32))
        wf = ctxC(nc.sbuf_tensor("at_wfc" + tag, [128, 512], F32))
        wbb = ctxC(nc.sbuf_tensor("at_wbc" + tag, [128, 512], BF16))
        xr = ctxC(nc.sbuf_tensor("at_xrc" + tag, [128, 512], F32))
        o_sb = ctxC(nc.sbuf_tensor("at_osbc" + tag, [128, 512], F32))
        pso = [ctxC(nc.psum_tensor(f"at_psoc{i}" + tag, [128, 512], F32)) for i in range(8)]
        S.do(nc.vector, lambda: nc.vector.memset(ones_r2[:], 1.0))
        S.do(nc.sync, lambda: nc.sync.dma_start(bo_sb[:], bo[None, :]), 16)
        for ob in range(8):
            S.chain(nc.tensor, [lambda ob=ob: nc.tensor.matmul(
                pso[0][:], ones_r2[:], bo_sb[:, 512 * ob:512 * (ob + 1)],
                start=True, stop=True)])
            S.do(nc.vector, lambda ob=ob: nc.vector.tensor_copy(
                bias_bc[:, 512 * ob:512 * (ob + 1)], pso[0][:]))
        for hp in range(nq // 1024):
            for hh in range(32):
                S.do(nc.sync, lambda hh=hh, hp=hp: nc.sync.dma_start(
                    cfm[:, hh * 1024:(hh + 1) * 1024],
                    ctx_hbm[128 * hh:128 * (hh + 1), hp * 1024:(hp + 1) * 1024]), 16)
            for ob in range(8):
                for dc in range(32):
                    S.do(nc.sync, lambda dc=dc, ob=ob: nc.sync.dma_start(
                        wf[:], woT[128 * dc:128 * (dc + 1), 512 * ob:512 * (ob + 1)]), 16)
                    S.do(nc.scalar, lambda: nc.scalar.copy(wbb[:], wf[:]))
                    S.chain(nc.tensor, [
                        (lambda tt=tt, dc=dc: nc.tensor.matmul(
                            pso[tt][:], cfm[:, dc * 1024 + 128 * tt: dc * 1024 + 128 * (tt + 1)],
                            wbb[:], start=(dc == 0), stop=(dc == 31), skip_group_check=True))
                        for tt in range(8)])
                for tt in range(8):
                    row = hp * 1024 + 128 * tt
                    S.do(nc.sync, lambda row=row, ob=ob: nc.sync.dma_start(
                        xr[:], x[row:row + 128, 512 * ob:512 * (ob + 1)]), 16)
                    S.do(nc.vector, lambda tt=tt: nc.vector.scalar_tensor_tensor(
                        o_sb[:], pso[tt][:], 1.0, xr[:], ALU.mult, ALU.add))
                    S.do(nc.vector, lambda ob=ob: nc.vector.tensor_add(
                        o_sb[:], o_sb[:], bias_bc[:, 512 * ob:512 * (ob + 1)]))
                    S.do(nc.sync, lambda row=row, ob=ob: nc.sync.dma_start(
                        dst[row:row + 128, 512 * ob:512 * (ob + 1)], o_sb[:]), 16)


def build_mega():
    nc = bass.Bass("TRN2", target_bir_lowering=False)
    ap = lambda t: t.ap()
    x0 = nc.dram_tensor("x0", [SEQ, D], F32, kind="ExternalInput").ap()
    bwT = nc.dram_tensor("bwT", [D, D], F32, kind="ExternalInput").ap()
    gwT = nc.dram_tensor("gwT", [D, 28], F32, kind="ExternalInput").ap()
    aaT = nc.dram_tensor("aaT", [D, 224], F32, kind="ExternalInput").ap()
    bal = nc.dram_tensor("bal", [224, D], F32, kind="ExternalInput").ap()
    erep = nc.dram_tensor("erep", [28, 224], F32, kind="ExternalInput").ap()
    wqT0 = nc.dram_tensor("wqT0", [D, 4096], F32, kind="ExternalInput").ap()
    wkT0 = nc.dram_tensor("wkT0", [D, 1024], F32, kind="ExternalInput").ap()
    wvT0 = nc.dram_tensor("wvT0", [D, 1024], F32, kind="ExternalInput").ap()
    woT0 = nc.dram_tensor("woT0", [D, 4096], F32, kind="ExternalInput").ap()
    bqkv0 = nc.dram_tensor("bqkv0", [128, 48], F32, kind="ExternalInput").ap()
    bo0 = nc.dram_tensor("bo0", [4096], F32, kind="ExternalInput").ap()
    cos0 = nc.dram_tensor("cos0", [128, SEQ], F32, kind="ExternalInput").ap()
    sin0 = nc.dram_tensor("sin0", [128, SEQ], F32, kind="ExternalInput").ap()
    mask0T = nc.dram_tensor("mask0T", [SEQ, SEQ], F32, kind="ExternalInput").ap()
    wqT1 = nc.dram_tensor("wqT1", [D, 4096], F32, kind="ExternalInput").ap()
    wkT1 = nc.dram_tensor("wkT1", [D, 1024], F32, kind="ExternalInput").ap()
    wvT1 = nc.dram_tensor("wvT1", [D, 1024], F32, kind="ExternalInput").ap()
    woT1 = nc.dram_tensor("woT1", [D, 4096], F32, kind="ExternalInput").ap()
    bqkv1 = nc.dram_tensor("bqkv1", [128, 48], F32, kind="ExternalInput").ap()
    bo1 = nc.dram_tensor("bo1", [4096], F32, kind="ExternalInput").ap()
    cos1 = nc.dram_tensor("cos1", [128, SEQ], F32, kind="ExternalInput").ap()
    sin1 = nc.dram_tensor("sin1", [128, SEQ], F32, kind="ExternalInput").ap()
    mask1T = nc.dram_tensor("mask1T", [SEQ, TPC], F32, kind="ExternalInput").ap()
    pk1 = nc.dram_tensor("pk1", [HC1, 128, 2048], I32, kind="ExternalInput").ap()
    s1 = nc.dram_tensor("s1", [HC1, 128, 64], F32, kind="ExternalInput").ap()
    pk3 = nc.dram_tensor("pk3", [HC1, 128, 2048], I32, kind="ExternalInput").ap()
    s3 = nc.dram_tensor("s3", [HC1, 128, 64], F32, kind="ExternalInput").ap()
    pk2 = nc.dram_tensor("pk2", [32, 128, 5504], I32, kind="ExternalInput").ap()
    s2 = nc.dram_tensor("s2", [32, 128, 172], F32, kind="ExternalInput").ap()
    xout = nc.dram_tensor("xout", [TPC, D], F32, kind="ExternalOutput").ap()
    w1s = nc.dram_tensor("w1s", [HC1, 128, 4096], BF16).ap()
    w3s = nc.dram_tensor("w3s", [HC1, 128, 4096], BF16).ap()
    w2s = nc.dram_tensor("w2s", [H1, 4096], BF16).ap()
    x1 = nc.dram_tensor("x1", [SEQ, D], F32).ap()
    x2 = nc.dram_tensor("x2", [SEQ, D], F32).ap()
    x3 = nc.dram_tensor("x3", [SEQ, D], F32).ap()
    x4 = nc.dram_tensor("x4", [SEQ, D], F32).ap()
    x5 = nc.dram_tensor("x5", [TPC, D], F32).ap()
    qh0 = nc.dram_tensor("qh0", [4096, SEQ], BF16).ap()
    ch0 = nc.dram_tensor("ch0", [4096, SEQ], BF16).ap()
    qh1 = nc.dram_tensor("qh1", [4096, TPC], BF16).ap()
    ch1 = nc.dram_tensor("ch1", [4096, TPC], BF16).ap()

    from contextlib import ExitStack
    with ExitStack() as st:
        ctx = st.enter_context
        S = Ser(nc)
        st.enter_context(nc.Block())
        idn = make_identity(nc, S, ctx)
        W0 = (bwT, gwT, aaT, bal, erep)
        emit_dequant(nc, S, st, pk1, s1, pk3, s3, pk2, s2, w1s, w3s, w2s)
        emit_ffn0(nc, S, idn, x0, x1, 0, W0, "a0")
        emit_ffn0(nc, S, idn, x0, x1, 1024, W0, "a1")
        emit_attn(nc, S, idn, dict(
            x=x1, dst=x2, nq=2048, wqT=wqT0, wkT=wkT0, wvT=wvT0, woT=woT0,
            bqkv=bqkv0, bo=bo0, cos=cos0, sin=sin0, maskT=mask0T,
            scale=1.0 / 128.0, q_hbm=qh0, ctx_hbm=ch0, tag="0"))
        emit_ffn0(nc, S, idn, x2, x3, 0, W0, "b0")
        emit_ffn0(nc, S, idn, x2, x3, 1024, W0, "b1")
        emit_ffn1(nc, S, idn, x3, x4, 0, w1s, w3s, w2s, "a0")
        emit_ffn1(nc, S, idn, x3, x4, 1024, w1s, w3s, w2s, "a1")
        emit_attn(nc, S, idn, dict(
            x=x4, dst=x5, nq=1024, wqT=wqT1, wkT=wkT1, wvT=wvT1, woT=woT1,
            bqkv=bqkv1, bo=bo1, cos=cos1, sin=sin1, maskT=mask1T,
            scale=128.0 ** -0.5, q_hbm=qh1, ctx_hbm=ch1, tag="1"))
        emit_ffn1(nc, S, idn, x5, xout, 0, w1s, w3s, w2s, "b0")
        S.fence(nc.sync)
    return nc


_cache = {}


def _get(name, builder):
    if name not in _cache:
        _cache[name] = builder()
    return _cache[name]


def _rope_tables(theta, positions):
    half = 64
    inv = 1.0 / (theta ** (np.arange(half, dtype=np.float64) / half))
    ang = positions[:, None].astype(np.float64) * inv[None, :]
    cos = np.cos(ang).T.astype(np.float32)
    sin = np.sin(ang).T.astype(np.float32)
    return (np.concatenate([cos, cos], 0), np.concatenate([sin, sin], 0))


def _l0_mask():
    BLOCK, LOCAL, VERT = 64, 16, 8
    nb = SEQ // BLOCK
    qb = np.arange(nb)[:, None]
    kb = np.arange(nb)[None, :]
    bm = (kb <= qb) & ((qb - kb < LOCAL) | ((kb + 1) % VERT == 0))
    em = np.kron(bm, np.ones((BLOCK, BLOCK), dtype=bool))
    i = np.arange(SEQ)
    em &= i[:, None] >= i[None, :]
    return em


def _l1_mask():
    i = np.arange(SEQ)
    return (i[None, :] <= i[:, None]) & (i[None, :] > i[:, None] - 1024)


def kernel(L0_attn_hidden_states, L0_attn_w_qkv, L0_attn_b_qkv, L0_attn_w_dense,
           L0_attn_b_dense, L0_ffn_base_W, L0_ffn_gate_W, L0_ffn_lora_A, L0_ffn_lora_B,
           L1_attn_wq, L1_attn_wk, L1_attn_wv, L1_attn_wo,
           L1_ffn_w1, L1_ffn_s1, L1_ffn_w2, L1_ffn_s2, L1_ffn_w3, L1_ffn_s3):
    f32 = np.float32
    xf = np.ascontiguousarray(np.asarray(L0_attn_hidden_states, dtype=f32).reshape(B, SEQ, D))

    bwT = np.ascontiguousarray(np.asarray(L0_ffn_base_W, f32).T)
    gwT = np.ascontiguousarray(np.asarray(L0_ffn_gate_W, f32).T)
    aaT = np.ascontiguousarray(np.asarray(L0_ffn_lora_A, f32).reshape(224, D).T)
    bal = np.ascontiguousarray(np.asarray(L0_ffn_lora_B, f32).transpose(0, 2, 1).reshape(224, D))
    erep = np.zeros((28, 224), f32)
    for e in range(28):
        erep[e, e * 8:(e + 1) * 8] = 1.0

    wqkv = np.asarray(L0_attn_w_qkv, f32).reshape(8, 6, 128, D)
    wqT0 = np.ascontiguousarray(wqkv[:, 0:4].reshape(4096, D).T)
    wkT0 = np.ascontiguousarray(wqkv[:, 4].reshape(1024, D).T)
    wvT0 = np.ascontiguousarray(wqkv[:, 5].reshape(1024, D).T)
    bq = np.asarray(L0_attn_b_qkv, f32).reshape(8, 6, 128)
    bqkv0 = np.zeros((128, 48), f32)
    bqkv0[:, 0:32] = bq[:, 0:4].reshape(32, 128).T
    bqkv0[:, 32:40] = bq[:, 4].reshape(8, 128).T
    bqkv0[:, 40:48] = bq[:, 5].reshape(8, 128).T
    woT0 = np.ascontiguousarray(np.asarray(L0_attn_w_dense, f32).T)
    bo0 = np.asarray(L0_attn_b_dense, f32)

    wqT1 = np.ascontiguousarray(np.asarray(L1_attn_wq, f32))
    wkT1 = np.ascontiguousarray(np.asarray(L1_attn_wk, f32))
    wvT1 = np.ascontiguousarray(np.asarray(L1_attn_wv, f32))
    woT1 = np.ascontiguousarray(np.asarray(L1_attn_wo, f32))
    bqkv1 = np.zeros((128, 48), f32)
    bo1 = np.zeros(4096, f32)

    pk1 = np.ascontiguousarray(np.asarray(L1_ffn_w1, np.int32).reshape(HC1, 128, 2048))
    pk3 = np.ascontiguousarray(np.asarray(L1_ffn_w3, np.int32).reshape(HC1, 128, 2048))
    pk2 = np.ascontiguousarray(np.asarray(L1_ffn_w2, np.int32).reshape(32, 128, 5504))
    s1 = np.ascontiguousarray(np.asarray(L1_ffn_s1, f32).reshape(HC1, 128, 64))
    s3 = np.ascontiguousarray(np.asarray(L1_ffn_s3, f32).reshape(HC1, 128, 64))
    s2 = np.ascontiguousarray(np.asarray(L1_ffn_s2, f32).reshape(32, 128, 172))

    m0 = _l0_mask()
    m1 = _l1_mask()
    pos = np.arange(SEQ)
    maps = []
    perms = []
    for c in range(NCORE):
        b = c // 2
        hb = c % 2
        own = np.arange(1024 * hb, 1024 * (hb + 1))
        other = np.arange(1024 * (1 - hb), 1024 * (2 - hb))
        perm = np.concatenate([own, other])
        perms.append((b, own))
        c0, s0 = _rope_tables(1000000.0, pos[perm])
        c1, s1r_ = _rope_tables(10000.0, pos[perm])
        m0T = np.where(m0[np.ix_(perm, perm)], 0.0, NEG).astype(f32).T
        m1T = np.where(m1[np.ix_(own, perm)], 0.0, NEG).astype(f32).T
        maps.append({
            "x0": np.ascontiguousarray(xf[b][perm]),
            "bwT": bwT, "gwT": gwT, "aaT": aaT, "bal": bal, "erep": erep,
            "wqT0": wqT0, "wkT0": wkT0, "wvT0": wvT0, "woT0": woT0,
            "bqkv0": bqkv0, "bo0": bo0,
            "cos0": np.ascontiguousarray(c0), "sin0": np.ascontiguousarray(s0),
            "mask0T": np.ascontiguousarray(m0T),
            "wqT1": wqT1, "wkT1": wkT1, "wvT1": wvT1, "woT1": woT1,
            "bqkv1": bqkv1, "bo1": bo1,
            "cos1": np.ascontiguousarray(c1), "sin1": np.ascontiguousarray(s1r_),
            "mask1T": np.ascontiguousarray(m1T),
            "pk1": pk1, "s1": s1, "pk3": pk3, "s3": s3, "pk2": pk2, "s2": s2,
        })
    nc = _get("mega", build_mega)
    res = run_bass_kernel_spmd(nc, maps, core_ids=list(range(NCORE)))
    out = np.empty((B, SEQ, D), f32)
    for c in range(NCORE):
        b, own = perms[c]
        out[b][own] = res.results[c]["xout"]
    return out


# version 04 (final)
# speedup vs baseline: 1.2236x; 1.2236x over previous
import numpy as np
import ml_dtypes
import concourse.bass as bass
import concourse.mybir as mybir
from concourse.bass_utils import run_bass_kernel_spmd

F32 = mybir.dt.float32
BF16 = mybir.dt.bfloat16
I32 = mybir.dt.int32
AF = mybir.ActivationFunctionType
ALU = mybir.AluOpType
AX = mybir.AxisListType
BF = ml_dtypes.bfloat16

NCORE = 8
D = 4096
SEQ = 2048
B = 4
NTOK = B * SEQ
TPC = 1024
H1 = 11008
HC1 = H1 // 128
EPS = 1e-5
NEG = -1.0e9


class Ser:
    def __init__(self, nc, name="ser"):
        self.nc = nc
        self._cm = nc.semaphore(name)
        self.sem = self._cm.__enter__()
        self.v = 0

    def do(self, eng, fn, inc=1):
        eng.wait_ge(self.sem, self.v)
        inst = fn()
        inst.then_inc(self.sem, inc)
        self.v += inc
        return inst

    def chain(self, eng, fns, inc=1):
        eng.wait_ge(self.sem, self.v)
        for f in fns[:-1]:
            f()
        fns[-1]().then_inc(self.sem, inc)
        self.v += inc

    def fence(self, eng):
        eng.wait_ge(self.sem, self.v)


def make_identity(nc, S, ctx):
    iot = ctx(nc.sbuf_tensor("idn_iot", [128, 128], I32))
    iotc = ctx(nc.sbuf_tensor("idn_iotc", [128, 1], I32))
    iotf = ctx(nc.sbuf_tensor("idn_iotf", [128, 128], F32))
    iotcf = ctx(nc.sbuf_tensor("idn_iotcf", [128, 1], F32))
    idn = ctx(nc.sbuf_tensor("idn", [128, 128], BF16))
    S.do(nc.gpsimd, lambda: nc.gpsimd.iota(iot[:], pattern=[[1, 128]], channel_multiplier=0))
    S.do(nc.gpsimd, lambda: nc.gpsimd.iota(iotc[:], pattern=[[0, 1]], channel_multiplier=1))
    S.do(nc.vector, lambda: nc.vector.tensor_copy(iotf[:], iot[:]))
    S.do(nc.vector, lambda: nc.vector.tensor_copy(iotcf[:], iotc[:]))
    S.do(nc.vector, lambda: nc.vector.tensor_single_scalar(idn[:], iotf[:], iotcf[:], ALU.is_equal))
    return idn


def emit_ln_transpose(nc, S, x_dram, row0, ntile, xn_fm, col0, idn, d):
    xt = d["ln_xt"]
    xn_tm = d["ln_xn_tm"]
    s1 = d["ln_s1"]; ssq = d["ln_ssq"]; m = d["ln_m"]; var = d["ln_var"]
    t0 = d["ln_t0"]; rstd = d["ln_rstd"]; nmr = d["ln_nmr"]
    FM_W = xn_fm.shape[1] // 32
    for t in range(ntile):
        S.do(nc.sync, lambda t=t: nc.sync.dma_start(
            xt[:], x_dram[row0 + 128 * t: row0 + 128 * (t + 1), :]), 16)
        S.do(nc.vector, lambda: nc.vector.reduce_sum(s1[:], xt[:], axis=AX.X))
        S.do(nc.scalar, lambda: nc.scalar.activation(xn_tm[:], xt[:], AF.Square, accum_out=ssq[:]))
        S.do(nc.vector, lambda: nc.vector.tensor_single_scalar(m[:], s1[:], 1.0 / D, ALU.mult))
        S.do(nc.vector, lambda: nc.vector.tensor_single_scalar(var[:], ssq[:], 1.0 / D, ALU.mult))
        S.do(nc.vector, lambda: nc.vector.tensor_mul(t0[:], m[:], m[:]))
        S.do(nc.vector, lambda: nc.vector.tensor_sub(var[:], var[:], t0[:]))
        S.do(nc.vector, lambda: nc.vector.tensor_single_scalar(var[:], var[:], EPS, ALU.add))
        S.do(nc.scalar, lambda: nc.scalar.activation(t0[:], var[:], AF.Ln))
        S.do(nc.scalar, lambda: nc.scalar.activation(rstd[:], t0[:], AF.Exp, scale=-0.5))
        S.do(nc.vector, lambda: nc.vector.tensor_mul(nmr[:], m[:], rstd[:]))
        S.do(nc.vector, lambda: nc.vector.tensor_single_scalar(nmr[:], nmr[:], -1.0, ALU.mult))
        S.do(nc.scalar, lambda: nc.scalar.activation(xn_tm[:], xt[:], AF.Identity,
                                                     bias=nmr[:], scale=rstd[:]))
        xv = xn_fm.ap().rearrange("p (dc w) -> p dc w", w=FM_W)[
            :, :, col0 + 128 * t: col0 + 128 * (t + 1)]
        S.do(nc.sync, lambda xv=xv: nc.sync.dma_start_transpose(xv, xn_tm[:]), 16)


def alloc_ln_scratch(nc, ctx, tag=""):
    d = {}
    d["ln_xt"] = ctx(nc.sbuf_tensor("ln_xt" + tag, [128, 4096], F32))
    d["ln_xn_tm"] = ctx(nc.sbuf_tensor("ln_xn_tm" + tag, [128, 4096], BF16))
    for nm in ["ln_s1", "ln_ssq", "ln_m", "ln_var", "ln_t0", "ln_rstd", "ln_nmr"]:
        d[nm] = ctx(nc.sbuf_tensor(nm + tag, [128, 1], F32))
    return d


def emit_dequant(nc, S, st, pk1, s1, pk3, s3, pk2, s2, w1s, w3s, w2s):
    from contextlib import ExitStack
    with ExitStack() as stq:
        ctx = stq.enter_context
        pkt = ctx(nc.sbuf_tensor("dq_pkt", [128, 2048], I32))
        s_t = ctx(nc.sbuf_tensor("dq_st", [128, 64], F32))
        ms_i = ctx(nc.sbuf_tensor("dq_msi", [128, 2048], I32))
        ls_i = ctx(nc.sbuf_tensor("dq_lsi", [128, 2048], I32))
        ms_f = ctx(nc.sbuf_tensor("dq_msf", [128, 2048], F32))
        ls_f = ctx(nc.sbuf_tensor("dq_lsf", [128, 2048], F32))
        dqb = ctx(nc.sbuf_tensor("dq_dqb", [128, 4096], BF16))
        tb = ctx(nc.sbuf_tensor("dq_tb", [128, 32, 128], BF16))
        dq_e = dqb.ap().rearrange("p (g two v) -> p g two v", two=2, v=64)[:, :, 0, :]
        dq_o = dqb.ap().rearrange("p (g two v) -> p g two v", two=2, v=64)[:, :, 1, :]
        s_e = s_t.ap().rearrange("p (g two) -> p g two", two=2)[:, :, 0:1].to_broadcast((128, 32, 64))
        s_o = s_t.ap().rearrange("p (g two) -> p g two", two=2)[:, :, 1:2].to_broadcast((128, 32, 64))
        msv = ms_f.ap().rearrange("p (g v) -> p g v", v=64)
        lsv = ls_f.ap().rearrange("p (g v) -> p g v", v=64)
        for t in range(HC1):
            for pk, ss, stash in ((pk1, s1, w1s), (pk3, s3, w3s)):
                S.do(nc.sync, lambda pk=pk, t=t: nc.sync.dma_start(pkt[:], pk[t]), 16)
                S.do(nc.sync, lambda ss=ss, t=t: nc.sync.dma_start(s_t[:], ss[t]), 16)
                S.do(nc.vector, lambda: nc.vector.tensor_single_scalar(ms_i[:], pkt[:], 4, ALU.arith_shift_right))
                S.do(nc.vector, lambda: nc.vector.tensor_scalar(ls_i[:], pkt[:], 15, 8, op0=ALU.bitwise_and, op1=ALU.bitwise_xor))
                S.do(nc.vector, lambda: nc.vector.tensor_single_scalar(ls_i[:], ls_i[:], 8, ALU.subtract))
                S.do(nc.vector, lambda: nc.vector.tensor_copy(ms_f[:], ms_i[:]))
                S.do(nc.vector, lambda: nc.vector.tensor_copy(ls_f[:], ls_i[:]))
                S.do(nc.vector, lambda: nc.vector.tensor_tensor(out=dq_e, in0=msv, in1=s_e, op=ALU.mult))
                S.do(nc.vector, lambda: nc.vector.tensor_tensor(out=dq_o, in0=lsv, in1=s_o, op=ALU.mult))
                S.do(nc.sync, lambda: nc.sync.dma_start_transpose(tb[:], dqb[:]), 16)
                S.do(nc.sync, lambda stash=stash, t=t: nc.sync.dma_start(
                    stash[t], tb.ap().rearrange("p a b -> p (a b)")), 16)
    with ExitStack() as stq:
        ctx = stq.enter_context
        pkt2 = ctx(nc.sbuf_tensor("dq_pkt2", [128, 5504], I32))
        s2t = ctx(nc.sbuf_tensor("dq_s2t", [128, 172], F32))
        m2i = ctx(nc.sbuf_tensor("dq_m2i", [128, 5504], I32))
        m2f = ctx(nc.sbuf_tensor("dq_m2f", [128, 5504], F32))
        l2f = ctx(nc.sbuf_tensor("dq_l2f", [128, 5504], F32))
        dq2 = ctx(nc.sbuf_tensor("dq_dq2", [128, 11008], BF16))
        tb2 = ctx(nc.sbuf_tensor("dq_tb2", [128, 86, 128], BF16))
        d2_e = dq2.ap().rearrange("p (g two v) -> p g two v", two=2, v=64)[:, :, 0, :]
        d2_o = dq2.ap().rearrange("p (g two v) -> p g two v", two=2, v=64)[:, :, 1, :]
        s2_e = s2t.ap().rearrange("p (g two) -> p g two", two=2)[:, :, 0:1].to_broadcast((128, 86, 64))
        s2_o = s2t.ap().rearrange("p (g two) -> p g two", two=2)[:, :, 1:2].to_broadcast((128, 86, 64))
        m2v = m2f.ap().rearrange("p (g v) -> p g v", v=64)
        l2v = l2f.ap().rearrange("p (g v) -> p g v", v=64)
        w2v = w2s.rearrange("(hc p) (rc d) -> p hc rc d", p=128, d=128)
        for rc in range(32):
            S.do(nc.sync, lambda rc=rc: nc.sync.dma_start(pkt2[:], pk2[rc]), 16)
            S.do(nc.sync, lambda rc=rc: nc.sync.dma_start(s2t[:], s2[rc]), 16)
            S.do(nc.vector, lambda: nc.vector.tensor_single_scalar(m2i[:], pkt2[:], 4, ALU.arith_shift_right))
            S.do(nc.vector, lambda: nc.vector.tensor_copy(m2f[:], m2i[:]))
            S.do(nc.vector, lambda: nc.vector.tensor_scalar(m2i[:], pkt2[:], 15, 8, op0=ALU.bitwise_and, op1=ALU.bitwise_xor))
            S.do(nc.vector, lambda: nc.vector.tensor_single_scalar(m2i[:], m2i[:], 8, ALU.subtract))
            S.do(nc.vector, lambda: nc.vector.tensor_copy(l2f[:], m2i[:]))
            S.do(nc.vector, lambda: nc.vector.tensor_tensor(out=d2_e, in0=m2v, in1=s2_e, op=ALU.mult))
            S.do(nc.vector, lambda: nc.vector.tensor_tensor(out=d2_o, in0=l2v, in1=s2_o, op=ALU.mult))
            S.do(nc.sync, lambda: nc.sync.dma_start_transpose(tb2[:], dq2[:]), 16)
            S.do(nc.sync, lambda rc=rc: nc.sync.dma_start(w2v[:, :, rc, :], tb2[:]), 16)


def emit_ffn0(nc, S, idn, x, xout, row0, W, tag):
    from contextlib import ExitStack
    bwT, gwT, aaT, bal, erep = W
    with ExitStack() as st:
        ctx = st.enter_context
        xn = ctx(nc.sbuf_tensor("f0_xn" + tag, [128, 32 * 1024], BF16))
        with ExitStack() as stA:
            lns = alloc_ln_scratch(nc, stA.enter_context, "f0" + tag)
            emit_ln_transpose(nc, S, x, row0, 8, xn, 0, idn, lns)
        gwf = ctx(nc.sbuf_tensor("f0_gwf" + tag, [128, 28], F32))
        gwb = ctx(nc.sbuf_tensor("f0_gwb" + tag, [128, 32 * 28], BF16))
        lg_fm = ctx(nc.sbuf_tensor("f0_lgfm" + tag, [28, 1024], BF16))
        lg_tm = ctx(nc.sbuf_tensor("f0_lgtm" + tag, [128, 8 * 28], F32))
        g_fm = ctx(nc.sbuf_tensor("f0_gfm" + tag, [28, 1024], BF16))
        grep = ctx(nc.sbuf_tensor("f0_grep" + tag, [128, 2 * 1024], F32))
        h_fm = ctx(nc.sbuf_tensor("f0_hfm" + tag, [128, 2 * 1024], BF16))
        stBC = st.enter_context(ExitStack())
        ctxB = stBC.enter_context
        ps_g = ctxB(nc.psum_tensor("f0_psg" + tag, [28, 512], F32))
        ps_l1 = ctxB(nc.psum_tensor("f0_psl1" + tag, [128, 28], BF16))
        ps_l2 = ctxB(nc.psum_tensor("f0_psl2" + tag, [28, 128], BF16))
        ps_r = ctxB(nc.psum_tensor("f0_psr" + tag, [128, 512], F32))
        for dc in range(32):
            S.do(nc.sync, lambda dc=dc: nc.sync.dma_start(
                gwf[:], gwT[128 * dc:128 * (dc + 1), :]), 16)
            S.do(nc.scalar, lambda dc=dc: nc.scalar.copy(gwb[:, 28 * dc:28 * (dc + 1)], gwf[:]))
        for th in range(2):
            S.chain(nc.tensor, [
                (lambda dc=dc, th=th: nc.tensor.matmul(
                    ps_g[:], gwb[:, 28 * dc:28 * (dc + 1)],
                    xn[:, dc * 1024 + th * 512: dc * 1024 + th * 512 + 512],
                    start=(dc == 0), stop=(dc == 31)))
                for dc in range(32)])
            S.do(nc.vector, lambda th=th: nc.vector.tensor_copy(
                lg_fm[:, th * 512:(th + 1) * 512], ps_g[:]))
        for tt in range(8):
            S.do(nc.tensor, lambda tt=tt: nc.tensor.transpose(
                ps_l1[:], lg_fm[:, 128 * tt:128 * (tt + 1)], idn[0:28, 0:28]))
            S.do(nc.vector, lambda tt=tt: nc.vector.tensor_copy(
                lg_tm[:, 28 * tt:28 * (tt + 1)], ps_l1[:]))
        top8 = ctxB(nc.sbuf_tensor("f0_top8" + tag, [128, 8], F32))
        nm1 = ctxB(nc.sbuf_tensor("f0_nm1" + tag, [128, 1], F32))
        em2 = ctxB(nc.sbuf_tensor("f0_em2" + tag, [128, 1], F32))
        rec2 = ctxB(nc.sbuf_tensor("f0_rec2" + tag, [128, 1], F32))
        e_t = ctxB(nc.sbuf_tensor("f0_et" + tag, [128, 28], F32))
        msk = ctxB(nc.sbuf_tensor("f0_msk" + tag, [128, 28], F32))
        g_tm = ctxB(nc.sbuf_tensor("f0_gtm" + tag, [128, 8 * 28], BF16))
        for tt in range(8):
            lg = lg_tm[:, 28 * tt:28 * (tt + 1)]
            S.do(nc.vector, lambda lg=lg: nc.vector.max(out=top8[:], in_=lg))
            S.do(nc.vector, lambda: nc.vector.tensor_single_scalar(nm1[:], top8[:, 0:1], -1.0, ALU.mult))
            S.do(nc.vector, lambda: nc.vector.tensor_sub(em2[:], top8[:, 1:2], top8[:, 0:1]))
            S.do(nc.scalar, lambda: nc.scalar.activation(em2[:], em2[:], AF.Exp))
            S.do(nc.vector, lambda: nc.vector.tensor_single_scalar(em2[:], em2[:], 1.0, ALU.add))
            S.do(nc.vector, lambda: nc.vector.reciprocal(rec2[:], em2[:]))
            S.do(nc.vector, lambda: nc.vector.tensor_single_scalar(rec2[:], rec2[:], 2.0, ALU.mult))
            S.do(nc.scalar, lambda lg=lg: nc.scalar.activation(e_t[:], lg, AF.Exp, bias=nm1[:]))
            S.do(nc.vector, lambda lg=lg: nc.vector.tensor_single_scalar(msk[:], lg, top8[:, 1:2], ALU.is_ge))
            S.do(nc.vector, lambda: nc.vector.tensor_mul(e_t[:], e_t[:], msk[:]))
            S.do(nc.vector, lambda tt=tt: nc.vector.tensor_single_scalar(
                g_tm[:, 28 * tt:28 * (tt + 1)], e_t[:], rec2[:], ALU.mult))
        for tt in range(8):
            S.do(nc.tensor, lambda tt=tt: nc.tensor.transpose(
                ps_l2[:], g_tm[:, 28 * tt:28 * (tt + 1)], idn[:]))
            S.do(nc.vector, lambda tt=tt: nc.vector.tensor_copy(
                g_fm[:, 128 * tt:128 * (tt + 1)], ps_l2[:]))
        erf = ctxB(nc.sbuf_tensor("f0_erf" + tag, [28, 224], F32))
        erb = ctxB(nc.sbuf_tensor("f0_erb" + tag, [28, 224], BF16))
        S.do(nc.sync, lambda: nc.sync.dma_start(erf[:], erep), 16)
        S.do(nc.scalar, lambda: nc.scalar.copy(erb[:], erf[:]))
        for mc, mw in ((0, 128), (1, 96)):
            for th in range(2):
                S.chain(nc.tensor, [lambda mc=mc, mw=mw, th=th: nc.tensor.matmul(
                    ps_r[0:mw, :], erb[:, 128 * mc:128 * mc + mw],
                    g_fm[:, th * 512:(th + 1) * 512], start=True, stop=True)])
                S.do(nc.vector, lambda mc=mc, mw=mw, th=th: nc.vector.tensor_copy(
                    grep[0:mw, mc * 1024 + th * 512: mc * 1024 + (th + 1) * 512], ps_r[0:mw, :]))
        aaf = ctxB(nc.sbuf_tensor("f0_aaf" + tag, [128, 224], F32))
        aab = ctxB(nc.sbuf_tensor("f0_aab" + tag, [128, 32 * 224], BF16))
        for dc in range(32):
            S.do(nc.sync, lambda dc=dc: nc.sync.dma_start(
                aaf[:], aaT[128 * dc:128 * (dc + 1), :]), 16)
            S.do(nc.scalar, lambda dc=dc: nc.scalar.copy(aab[:, 224 * dc:224 * (dc + 1)], aaf[:]))
        for mc, mw in ((0, 128), (1, 96)):
            for th in range(2):
                S.chain(nc.tensor, [
                    (lambda dc=dc, mc=mc, mw=mw, th=th: nc.tensor.matmul(
                        ps_r[0:mw, :], aab[:, 224 * dc + 128 * mc: 224 * dc + 128 * mc + mw],
                        xn[:, dc * 1024 + th * 512: dc * 1024 + th * 512 + 512],
                        start=(dc == 0), stop=(dc == 31)))
                    for dc in range(32)])
                S.do(nc.vector, lambda mc=mc, mw=mw, th=th: nc.vector.tensor_mul(
                    h_fm[0:mw, mc * 1024 + th * 512: mc * 1024 + (th + 1) * 512],
                    ps_r[0:mw, :], grep[0:mw, mc * 1024 + th * 512: mc * 1024 + (th + 1) * 512]))
        stBC.close()
        wf = ctx(nc.sbuf_tensor("f0_wf" + tag, [128, 512], F32))
        wb = ctx(nc.sbuf_tensor("f0_wb" + tag, [128, 512], BF16))
        bbf = ctx(nc.sbuf_tensor("f0_bbf" + tag, [128, 512], F32))
        bb0 = ctx(nc.sbuf_tensor("f0_bb0" + tag, [128, 512], BF16))
        bb1 = ctx(nc.sbuf_tensor("f0_bb1" + tag, [96, 512], BF16))
        xr = ctx(nc.sbuf_tensor("f0_xr" + tag, [128, 512], F32))
        o_sb = ctx(nc.sbuf_tensor("f0_osb" + tag, [128, 512], F32))
        pso = [ctx(nc.psum_tensor(f"f0_pso{i}" + tag, [128, 512], F32)) for i in range(8)]
        for ob in range(8):
            S.do(nc.sync, lambda ob=ob: nc.sync.dma_start(bbf[:], bal[0:128, 512 * ob:512 * (ob + 1)]), 16)
            S.do(nc.scalar, lambda: nc.scalar.copy(bb0[:], bbf[:]))
            S.do(nc.sync, lambda ob=ob: nc.sync.dma_start(bbf[0:96, :], bal[128:224, 512 * ob:512 * (ob + 1)]), 16)
            S.do(nc.scalar, lambda: nc.scalar.copy(bb1[:], bbf[0:96, :]))
            for dc in range(32):
                S.do(nc.sync, lambda dc=dc, ob=ob: nc.sync.dma_start(
                    wf[:], bwT[128 * dc:128 * (dc + 1), 512 * ob:512 * (ob + 1)]), 16)
                S.do(nc.scalar, lambda: nc.scalar.copy(wb[:], wf[:]))
                S.chain(nc.tensor, [
                    (lambda tt=tt, dc=dc: nc.tensor.matmul(
                        pso[tt][:], xn[:, dc * 1024 + 128 * tt: dc * 1024 + 128 * (tt + 1)],
                        wb[:], start=(dc == 0), stop=False, skip_group_check=True))
                    for tt in range(8)])
            S.chain(nc.tensor, [
                (lambda tt=tt: nc.tensor.matmul(
                    pso[tt][:], h_fm[:, 128 * tt:128 * (tt + 1)], bb0[:],
                    start=False, stop=False, skip_group_check=True))
                for tt in range(8)])
            S.chain(nc.tensor, [
                (lambda tt=tt: nc.tensor.matmul(
                    pso[tt][:], h_fm[0:96, 1024 + 128 * tt: 1024 + 128 * (tt + 1)], bb1[:],
                    start=False, stop=True, skip_group_check=True))
                for tt in range(8)])
            for tt in range(8):
                S.do(nc.sync, lambda tt=tt, ob=ob: nc.sync.dma_start(
                    xr[:], x[row0 + 128 * tt: row0 + 128 * (tt + 1), 512 * ob:512 * (ob + 1)]), 16)
                S.do(nc.vector, lambda tt=tt: nc.vector.scalar_tensor_tensor(
                    o_sb[:], pso[tt][:], 0.5, xr[:], ALU.mult, ALU.add))
                S.do(nc.sync, lambda tt=tt, ob=ob: nc.sync.dma_start(
                    xout[row0 + 128 * tt: row0 + 128 * (tt + 1), 512 * ob:512 * (ob + 1)], o_sb[:]), 16)


def emit_ffn1(nc, S, idn, x, xout, row0, w1s, w3s, w2s, tag, pp):
    from contextlib import ExitStack
    with ExitStack() as st:
        ctx = st.enter_context
        xn = ctx(nc.sbuf_tensor("f1_xn" + tag, [128, 32 * 1024], BF16))
        with ExitStack() as stA:
            lns = alloc_ln_scratch(nc, stA.enter_context, "f1" + tag)
            emit_ln_transpose(nc, S, x, row0, 8, xn, 0, idn, lns)
        g_buf = ctx(nc.sbuf_tensor("f1_gbuf" + tag, [128, HC1 * 512], BF16))
        w1sb = [ctx(nc.sbuf_tensor(f"f1_w1sb{i}" + tag, [128, 4096], BF16)) for i in range(2)]
        w3sb = [ctx(nc.sbuf_tensor(f"f1_w3sb{i}" + tag, [128, 4096], BF16)) for i in range(2)]
        a_s = ctx(nc.sbuf_tensor("f1_as" + tag, [128, 512], BF16))
        b_s = ctx(nc.sbuf_tensor("f1_bs" + tag, [128, 512], BF16))
        w2sb = [ctx(nc.sbuf_tensor(f"f1_w2sb{i}" + tag, [128, 512], BF16)) for i in range(2)]
        xr = ctx(nc.sbuf_tensor("f1_xr" + tag, [128, 512], F32))
        o_sb = ctx(nc.sbuf_tensor("f1_osb" + tag, [128, 512], F32))
        psA = ctx(nc.psum_tensor("f1_psA" + tag, [128, 512], F32))
        psB = ctx(nc.psum_tensor("f1_psB" + tag, [128, 512], F32))
        pso = [ctx(nc.psum_tensor(f"f1_pso{i}" + tag, [128, 512], F32)) for i in range(4)]
        sW, sM = pp["sW"], pp["sM"]
        for th in range(2):
            mhist = []
            whist = []
            for hc in range(HC1):
                bf = hc % 2
                if hc < 2:
                    nc.sync.wait_ge(S.sem, S.v)
                else:
                    nc.sync.wait_ge(sM, mhist[hc - 2])
                nc.sync.dma_start(w1sb[bf][:], w1s[hc]).then_inc(sW, 16)
                nc.sync.dma_start(w3sb[bf][:], w3s[hc]).then_inc(sW, 16)
                pp["W"] += 32
                whist.append(pp["W"])
                if hc == 0:
                    nc.tensor.wait_ge(S.sem, S.v)
                else:
                    nc.tensor.wait_ge(S.sem, S.v)
                nc.tensor.wait_ge(sW, whist[hc])
                for dc in range(32):
                    nc.tensor.matmul(
                        psA[:], w1sb[bf][:, 128 * dc:128 * (dc + 1)],
                        xn[:, dc * 1024 + th * 512: dc * 1024 + th * 512 + 512],
                        start=(dc == 0), stop=(dc == 31))
                for dc in range(32):
                    inst = nc.tensor.matmul(
                        psB[:], w3sb[bf][:, 128 * dc:128 * (dc + 1)],
                        xn[:, dc * 1024 + th * 512: dc * 1024 + th * 512 + 512],
                        start=(dc == 0), stop=(dc == 31))
                inst.then_inc(sM, 1)
                pp["M"] += 1
                mhist.append(pp["M"])
                nc.scalar.wait_ge(sM, pp["M"])
                S.do(nc.scalar, lambda: nc.scalar.activation(a_s[:], psA[:], AF.Silu))
                S.do(nc.vector, lambda: nc.vector.tensor_copy(b_s[:], psB[:]))
                S.do(nc.vector, lambda hc=hc: nc.vector.tensor_mul(
                    g_buf[:, hc * 512:(hc + 1) * 512], a_s[:], b_s[:]))
            for ob in range(8):
                m2hist = []
                for hc in range(HC1):
                    bf = hc % 2
                    if hc < 2:
                        nc.sync.wait_ge(S.sem, S.v)
                    else:
                        nc.sync.wait_ge(sM, m2hist[hc - 2])
                    nc.sync.dma_start(
                        w2sb[bf][:],
                        w2s[128 * hc:128 * (hc + 1), 512 * ob:512 * (ob + 1)]).then_inc(sW, 16)
                    pp["W"] += 16
                    w2w = pp["W"]
                    if hc == 0:
                        nc.tensor.wait_ge(S.sem, S.v)
                    nc.tensor.wait_ge(sW, w2w)
                    for tt in range(4):
                        inst = nc.tensor.matmul(
                            pso[tt][:], g_buf[:, hc * 512 + 128 * tt: hc * 512 + 128 * (tt + 1)],
                            w2sb[bf][:], start=(hc == 0), stop=(hc == HC1 - 1),
                            skip_group_check=True)
                    inst.then_inc(sM, 1)
                    pp["M"] += 1
                    m2hist.append(pp["M"])
                nc.vector.wait_ge(sM, pp["M"])
                for tt in range(4):
                    S.do(nc.sync, lambda tt=tt, th=th, ob=ob: nc.sync.dma_start(
                        xr[:], x[row0 + th * 512 + 128 * tt: row0 + th * 512 + 128 * (tt + 1),
                                 512 * ob:512 * (ob + 1)]), 16)
                    S.do(nc.vector, lambda tt=tt: nc.vector.scalar_tensor_tensor(
                        o_sb[:], pso[tt][:], 0.5, xr[:], ALU.mult, ALU.add))
                    S.do(nc.sync, lambda tt=tt, th=th, ob=ob: nc.sync.dma_start(
                        xout[row0 + th * 512 + 128 * tt: row0 + th * 512 + 128 * (tt + 1),
                             512 * ob:512 * (ob + 1)], o_sb[:]), 16)


def emit_attn(nc, S, idn, cfg):
    from contextlib import ExitStack
    x = cfg["x"]; dst = cfg["dst"]; nq = cfg["nq"]
    wqT = cfg["wqT"]; wkT = cfg["wkT"]; wvT = cfg["wvT"]; woT = cfg["woT"]
    bqkv = cfg["bqkv"]; bo = cfg["bo"]
    cos = cfg["cos"]; sin = cfg["sin"]; maskT = cfg["maskT"]
    scale = cfg["scale"]; q_hbm = cfg["q_hbm"]; ctx_hbm = cfg["ctx_hbm"]
    tag = cfg["tag"]
    nqh = nq // 512
    k_fm = None; v_tm = None
    with ExitStack() as st:
        ctx = st.enter_context
        k_fm = ctx(nc.sbuf_tensor("at_kfm" + tag, [128, 8 * 2048], BF16))
        v_tm = ctx(nc.sbuf_tensor("at_vtm" + tag, [128, 8 * 2048], BF16))
        with ExitStack() as stA:
            ctxA = stA.enter_context
            lns = alloc_ln_scratch(nc, ctxA, "at" + tag)
            xnb = ctxA(nc.sbuf_tensor("at_xnb" + tag, [128, 32 * 512], BF16))
            cos_s = ctxA(nc.sbuf_tensor("at_cos" + tag, [128, 512], F32))
            sin_s = ctxA(nc.sbuf_tensor("at_sin" + tag, [128, 512], F32))
            bsb = ctxA(nc.sbuf_tensor("at_bsb" + tag, [128, 48], F32))
            wf4 = ctxA(nc.sbuf_tensor("at_wf4" + tag, [128, 512], F32))
            wb4 = ctxA(nc.sbuf_tensor("at_wb4" + tag, [128, 512], BF16))
            kvf = ctxA(nc.sbuf_tensor("at_kvf" + tag, [128, 512], F32))
            rot = ctxA(nc.sbuf_tensor("at_rot" + tag, [128, 512], F32))
            vb = ctxA(nc.sbuf_tensor("at_vb" + tag, [128, 512], BF16))
            pk4 = [ctxA(nc.psum_tensor(f"at_pk{i}" + tag, [128, 512], F32)) for i in range(4)]
            pvt = ctxA(nc.psum_tensor("at_pvt" + tag, [128, 128], BF16))
            S.do(nc.sync, lambda: nc.sync.dma_start(bsb[:], bqkv), 16)

            def rope_emit(dst_bf):
                S.do(nc.vector, lambda: nc.vector.tensor_single_scalar(rot[0:64, :], kvf[64:128, :], -1.0, ALU.mult))
                S.do(nc.vector, lambda: nc.vector.tensor_copy(rot[64:128, :], kvf[0:64, :]))
                S.do(nc.vector, lambda: nc.vector.tensor_mul(kvf[:], kvf[:], cos_s[:]))
                S.do(nc.vector, lambda: nc.vector.tensor_mul(rot[:], rot[:], sin_s[:]))
                S.do(nc.vector, lambda: nc.vector.tensor_add(dst_bf, kvf[:], rot[:]))

            for tg in range(4):
                emit_ln_transpose(nc, S, x, tg * 512, 4, xnb, 0, idn, lns)
                S.do(nc.sync, lambda tg=tg: nc.sync.dma_start(cos_s[:], cos[:, tg * 512:(tg + 1) * 512]), 16)
                S.do(nc.sync, lambda tg=tg: nc.sync.dma_start(sin_s[:], sin[:, tg * 512:(tg + 1) * 512]), 16)

                def round4(w_dram, r):
                    for dc in range(32):
                        S.do(nc.sync, lambda dc=dc, r=r, w_dram=w_dram: nc.sync.dma_start(
                            wf4[:], w_dram[128 * dc:128 * (dc + 1), 512 * r:512 * (r + 1)]), 16)
                        S.do(nc.scalar, lambda: nc.scalar.copy(wb4[:], wf4[:]))
                        S.chain(nc.tensor, [
                            (lambda i=i, dc=dc: nc.tensor.matmul(
                                pk4[i][:], wb4[:, 128 * i:128 * (i + 1)],
                                xnb[:, dc * 512:(dc + 1) * 512],
                                start=(dc == 0), stop=(dc == 31), skip_group_check=True))
                            for i in range(4)])

                for r in range(2):
                    round4(wkT, r)
                    for i in range(4):
                        g = 4 * r + i
                        S.do(nc.scalar, lambda i=i, g=g: nc.scalar.activation(
                            kvf[:], pk4[i][:], AF.Identity, bias=bsb[:, 32 + g:33 + g]))
                        rope_emit(k_fm[:, g * 2048 + tg * 512: g * 2048 + (tg + 1) * 512])
                for r in range(2):
                    round4(wvT, r)
                    for i in range(4):
                        g = 4 * r + i
                        S.do(nc.scalar, lambda i=i, g=g: nc.scalar.activation(
                            vb[:], pk4[i][:], AF.Identity, bias=bsb[:, 40 + g:41 + g]))
                        for j in range(4):
                            S.do(nc.tensor, lambda j=j: nc.tensor.transpose(
                                pvt[:], vb[:, 128 * j:128 * (j + 1)], idn[:]))
                            S.do(nc.vector, lambda j=j, g=g, tg=tg: nc.vector.tensor_copy(
                                v_tm[:, (g * 16 + tg * 4 + j) * 128:(g * 16 + tg * 4 + j) * 128 + 128],
                                pvt[:]))
                if tg * 512 < nq:
                    for r in range(8):
                        round4(wqT, r)
                        for i in range(4):
                            h = 4 * r + i
                            S.do(nc.scalar, lambda i=i, h=h: nc.scalar.activation(
                                kvf[:], pk4[i][:], AF.Identity, bias=bsb[:, h:h + 1]))
                            rope_emit(vb[:])
                            S.do(nc.sync, lambda h=h, tg=tg: nc.sync.dma_start(
                                q_hbm[128 * h:128 * (h + 1), tg * 512:(tg + 1) * 512], vb[:]), 16)
        with ExitStack() as stB:
            ctxB = stB.enter_context
            mkq = ctxB(nc.sbuf_tensor("at_mkq" + tag, [128, 16 * 512], F32))
            ones_c = ctxB(nc.sbuf_tensor("at_onc" + tag, [128, 1], BF16))
            ones_r = ctxB(nc.sbuf_tensor("at_onr" + tag, [1, 128], F32))
            qh_sb = ctxB(nc.sbuf_tensor("at_qh" + tag, [128, 512], BF16))
            p_T = [ctxB(nc.sbuf_tensor(f"at_pT{i}" + tag, [128, 512], BF16)) for i in range(2)]
            rec_d = ctxB(nc.sbuf_tensor("at_rec" + tag, [1, 512], F32))
            bc_sb = ctxB(nc.sbuf_tensor("at_bc" + tag, [128, 512], F32))
            ctx_sb = ctxB(nc.sbuf_tensor("at_ctx" + tag, [128, 512], BF16))
            ps_s = [ctxB(nc.psum_tensor(f"at_pss{i}" + tag, [128, 512], F32)) for i in range(2)]
            ps_c = ctxB(nc.psum_tensor("at_psc" + tag, [128, 512], F32))
            ps_d = ctxB(nc.psum_tensor("at_psd" + tag, [1, 512], F32))
            ps_b = ctxB(nc.psum_tensor("at_psb" + tag, [128, 512], F32))
            S.do(nc.vector, lambda: nc.vector.memset(ones_c[:], 1.0))
            S.do(nc.vector, lambda: nc.vector.memset(ones_r[:], 1.0))
            sP = st.enter_context(nc.semaphore("sP" + tag))
            sV = st.enter_context(nc.semaphore("sV" + tag))
            sX = st.enter_context(nc.semaphore("sX" + tag))
            sC = st.enter_context(nc.semaphore("sC" + tag))
            cnt = {"P": 0, "V": 0, "X": 0, "C": 0}
            for qh in range(nqh):
                nc.sync.wait_ge(sV, cnt["V"])
                for kt in range(16):
                    S.do(nc.sync, lambda kt=kt, qh=qh: nc.sync.dma_start(
                        mkq[:, kt * 512:(kt + 1) * 512],
                        maskT[128 * kt:128 * (kt + 1), qh * 512:(qh + 1) * 512]), 16)
                for h in range(32):
                    g = h // 4
                    nc.sync.wait_ge(sP, cnt["P"])
                    S.do(nc.sync, lambda h=h, qh=qh: nc.sync.dma_start(
                        qh_sb[:], q_hbm[128 * h:128 * (h + 1), qh * 512:(qh + 1) * 512]), 16)
                    for kt in range(16):
                        b = kt % 2
                        S.chain(nc.tensor, [lambda b=b, g=g, kt=kt: nc.tensor.matmul(
                            ps_s[b][:], k_fm[:, g * 2048 + kt * 128: g * 2048 + kt * 128 + 128],
                            qh_sb[:], start=True, stop=True, skip_group_check=True)])
                        S.do(nc.vector, lambda b=b, kt=kt: nc.vector.tensor_add(
                            ps_s[b][:], ps_s[b][:], mkq[:, kt * 512:(kt + 1) * 512]))
                        S.do(nc.scalar, lambda b=b: nc.scalar.activation(
                            p_T[b][:], ps_s[b][:], AF.Exp, scale=scale))
                        S.chain(nc.tensor, [
                            (lambda b=b, g=g, kt=kt: nc.tensor.matmul(
                                ps_c[:], v_tm[:, (g * 16 + kt) * 128:(g * 16 + kt) * 128 + 128],
                                p_T[b][:], start=(kt == 0), stop=(kt == 15), skip_group_check=True)),
                            (lambda b=b, kt=kt: nc.tensor.matmul(
                                ps_d[:], ones_c[:], p_T[b][:],
                                start=(kt == 0), stop=(kt == 15), skip_group_check=True))])
                    S.do(nc.vector, lambda: nc.vector.reciprocal(rec_d[:], ps_d[:]))
                    S.chain(nc.tensor, [lambda: nc.tensor.matmul(
                        ps_b[:], ones_r[:], rec_d[:], start=True, stop=True, skip_group_check=True)])
                    S.do(nc.vector, lambda: nc.vector.tensor_copy(bc_sb[:], ps_b[:]))
                    S.do(nc.vector, lambda: nc.vector.tensor_mul(ctx_sb[:], ps_c[:], bc_sb[:]))
                    S.do(nc.sync, lambda h=h, qh=qh: nc.sync.dma_start(
                        ctx_hbm[128 * h:128 * (h + 1), qh * 512:(qh + 1) * 512], ctx_sb[:]), 16)
    from contextlib import ExitStack as ES2
    with ES2() as stC:
        ctxC = stC.enter_context
        cfm = ctxC(nc.sbuf_tensor("at_cfm" + tag, [128, 32 * 1024], BF16))
        bo_sb = ctxC(nc.sbuf_tensor("at_bo" + tag, [1, 4096], F32))
        bias_bc = ctxC(nc.sbuf_tensor("at_bbc" + tag, [128, 4096], F32))
        ones_r2 = ctxC(nc.sbuf_tensor("at_onr2" + tag, [1, 128], F32))
        wf = ctxC(nc.sbuf_tensor("at_wfc" + tag, [128, 512], F32))
        wbb = ctxC(nc.sbuf_tensor("at_wbc" + tag, [128, 512], BF16))
        xr = ctxC(nc.sbuf_tensor("at_xrc" + tag, [128, 512], F32))
        o_sb = ctxC(nc.sbuf_tensor("at_osbc" + tag, [128, 512], F32))
        pso = [ctxC(nc.psum_tensor(f"at_psoc{i}" + tag, [128, 512], F32)) for i in range(8)]
        S.do(nc.vector, lambda: nc.vector.memset(ones_r2[:], 1.0))
        S.do(nc.sync, lambda: nc.sync.dma_start(bo_sb[:], bo[None, :]), 16)
        for ob in range(8):
            S.chain(nc.tensor, [lambda ob=ob: nc.tensor.matmul(
                pso[0][:], ones_r2[:], bo_sb[:, 512 * ob:512 * (ob + 1)],
                start=True, stop=True)])
            S.do(nc.vector, lambda ob=ob: nc.vector.tensor_copy(
                bias_bc[:, 512 * ob:512 * (ob + 1)], pso[0][:]))
        for hp in range(nq // 1024):
            for hh in range(32):
                S.do(nc.sync, lambda hh=hh, hp=hp: nc.sync.dma_start(
                    cfm[:, hh * 1024:(hh + 1) * 1024],
                    ctx_hbm[128 * hh:128 * (hh + 1), hp * 1024:(hp + 1) * 1024]), 16)
            for ob in range(8):
                for dc in range(32):
                    S.do(nc.sync, lambda dc=dc, ob=ob: nc.sync.dma_start(
                        wf[:], woT[128 * dc:128 * (dc + 1), 512 * ob:512 * (ob + 1)]), 16)
                    S.do(nc.scalar, lambda: nc.scalar.copy(wbb[:], wf[:]))
                    S.chain(nc.tensor, [
                        (lambda tt=tt, dc=dc: nc.tensor.matmul(
                            pso[tt][:], cfm[:, dc * 1024 + 128 * tt: dc * 1024 + 128 * (tt + 1)],
                            wbb[:], start=(dc == 0), stop=(dc == 31), skip_group_check=True))
                        for tt in range(8)])
                for tt in range(8):
                    row = hp * 1024 + 128 * tt
                    S.do(nc.sync, lambda row=row, ob=ob: nc.sync.dma_start(
                        xr[:], x[row:row + 128, 512 * ob:512 * (ob + 1)]), 16)
                    S.do(nc.vector, lambda tt=tt: nc.vector.scalar_tensor_tensor(
                        o_sb[:], pso[tt][:], 1.0, xr[:], ALU.mult, ALU.add))
                    S.do(nc.vector, lambda ob=ob: nc.vector.tensor_add(
                        o_sb[:], o_sb[:], bias_bc[:, 512 * ob:512 * (ob + 1)]))
                    S.do(nc.sync, lambda row=row, ob=ob: nc.sync.dma_start(
                        dst[row:row + 128, 512 * ob:512 * (ob + 1)], o_sb[:]), 16)


def build_mega():
    nc = bass.Bass("TRN2", target_bir_lowering=False)
    ap = lambda t: t.ap()
    x0 = nc.dram_tensor("x0", [SEQ, D], F32, kind="ExternalInput").ap()
    bwT = nc.dram_tensor("bwT", [D, D], F32, kind="ExternalInput").ap()
    gwT = nc.dram_tensor("gwT", [D, 28], F32, kind="ExternalInput").ap()
    aaT = nc.dram_tensor("aaT", [D, 224], F32, kind="ExternalInput").ap()
    bal = nc.dram_tensor("bal", [224, D], F32, kind="ExternalInput").ap()
    erep = nc.dram_tensor("erep", [28, 224], F32, kind="ExternalInput").ap()
    wqT0 = nc.dram_tensor("wqT0", [D, 4096], F32, kind="ExternalInput").ap()
    wkT0 = nc.dram_tensor("wkT0", [D, 1024], F32, kind="ExternalInput").ap()
    wvT0 = nc.dram_tensor("wvT0", [D, 1024], F32, kind="ExternalInput").ap()
    woT0 = nc.dram_tensor("woT0", [D, 4096], F32, kind="ExternalInput").ap()
    bqkv0 = nc.dram_tensor("bqkv0", [128, 48], F32, kind="ExternalInput").ap()
    bo0 = nc.dram_tensor("bo0", [4096], F32, kind="ExternalInput").ap()
    cos0 = nc.dram_tensor("cos0", [128, SEQ], F32, kind="ExternalInput").ap()
    sin0 = nc.dram_tensor("sin0", [128, SEQ], F32, kind="ExternalInput").ap()
    mask0T = nc.dram_tensor("mask0T", [SEQ, SEQ], F32, kind="ExternalInput").ap()
    wqT1 = nc.dram_tensor("wqT1", [D, 4096], F32, kind="ExternalInput").ap()
    wkT1 = nc.dram_tensor("wkT1", [D, 1024], F32, kind="ExternalInput").ap()
    wvT1 = nc.dram_tensor("wvT1", [D, 1024], F32, kind="ExternalInput").ap()
    woT1 = nc.dram_tensor("woT1", [D, 4096], F32, kind="ExternalInput").ap()
    bqkv1 = nc.dram_tensor("bqkv1", [128, 48], F32, kind="ExternalInput").ap()
    bo1 = nc.dram_tensor("bo1", [4096], F32, kind="ExternalInput").ap()
    cos1 = nc.dram_tensor("cos1", [128, SEQ], F32, kind="ExternalInput").ap()
    sin1 = nc.dram_tensor("sin1", [128, SEQ], F32, kind="ExternalInput").ap()
    mask1T = nc.dram_tensor("mask1T", [SEQ, TPC], F32, kind="ExternalInput").ap()
    pk1 = nc.dram_tensor("pk1", [HC1, 128, 2048], I32, kind="ExternalInput").ap()
    s1 = nc.dram_tensor("s1", [HC1, 128, 64], F32, kind="ExternalInput").ap()
    pk3 = nc.dram_tensor("pk3", [HC1, 128, 2048], I32, kind="ExternalInput").ap()
    s3 = nc.dram_tensor("s3", [HC1, 128, 64], F32, kind="ExternalInput").ap()
    pk2 = nc.dram_tensor("pk2", [32, 128, 5504], I32, kind="ExternalInput").ap()
    s2 = nc.dram_tensor("s2", [32, 128, 172], F32, kind="ExternalInput").ap()
    xout = nc.dram_tensor("xout", [TPC, D], F32, kind="ExternalOutput").ap()
    w1s = nc.dram_tensor("w1s", [HC1, 128, 4096], BF16).ap()
    w3s = nc.dram_tensor("w3s", [HC1, 128, 4096], BF16).ap()
    w2s = nc.dram_tensor("w2s", [H1, 4096], BF16).ap()
    x1 = nc.dram_tensor("x1", [SEQ, D], F32).ap()
    x2 = nc.dram_tensor("x2", [SEQ, D], F32).ap()
    x3 = nc.dram_tensor("x3", [SEQ, D], F32).ap()
    x4 = nc.dram_tensor("x4", [SEQ, D], F32).ap()
    x5 = nc.dram_tensor("x5", [TPC, D], F32).ap()
    qh0 = nc.dram_tensor("qh0", [4096, SEQ], BF16).ap()
    ch0 = nc.dram_tensor("ch0", [4096, SEQ], BF16).ap()
    qh1 = nc.dram_tensor("qh1", [4096, TPC], BF16).ap()
    ch1 = nc.dram_tensor("ch1", [4096, TPC], BF16).ap()

    from contextlib import ExitStack
    with ExitStack() as st:
        ctx = st.enter_context
        S = Ser(nc)
        st.enter_context(nc.Block())
        idn = make_identity(nc, S, ctx)
        pp = {"sW": ctx(nc.semaphore("ppW")), "sM": ctx(nc.semaphore("ppM")),
              "W": 0, "M": 0}
        W0 = (bwT, gwT, aaT, bal, erep)
        emit_dequant(nc, S, st, pk1, s1, pk3, s3, pk2, s2, w1s, w3s, w2s)
        emit_ffn0(nc, S, idn, x0, x1, 0, W0, "a0")
        emit_ffn0(nc, S, idn, x0, x1, 1024, W0, "a1")
        emit_attn(nc, S, idn, dict(
            x=x1, dst=x2, nq=2048, wqT=wqT0, wkT=wkT0, wvT=wvT0, woT=woT0,
            bqkv=bqkv0, bo=bo0, cos=cos0, sin=sin0, maskT=mask0T,
            scale=1.0 / 128.0, q_hbm=qh0, ctx_hbm=ch0, tag="0"))
        emit_ffn0(nc, S, idn, x2, x3, 0, W0, "b0")
        emit_ffn0(nc, S, idn, x2, x3, 1024, W0, "b1")
        emit_ffn1(nc, S, idn, x3, x4, 0, w1s, w3s, w2s, "a0", pp)
        emit_ffn1(nc, S, idn, x3, x4, 1024, w1s, w3s, w2s, "a1", pp)
        emit_attn(nc, S, idn, dict(
            x=x4, dst=x5, nq=1024, wqT=wqT1, wkT=wkT1, wvT=wvT1, woT=woT1,
            bqkv=bqkv1, bo=bo1, cos=cos1, sin=sin1, maskT=mask1T,
            scale=128.0 ** -0.5, q_hbm=qh1, ctx_hbm=ch1, tag="1"))
        emit_ffn1(nc, S, idn, x5, xout, 0, w1s, w3s, w2s, "b0", pp)
        S.fence(nc.sync)
    return nc


_cache = {}


def _get(name, builder):
    if name not in _cache:
        _cache[name] = builder()
    return _cache[name]


def _rope_tables(theta, positions):
    half = 64
    inv = 1.0 / (theta ** (np.arange(half, dtype=np.float64) / half))
    ang = positions[:, None].astype(np.float64) * inv[None, :]
    cos = np.cos(ang).T.astype(np.float32)
    sin = np.sin(ang).T.astype(np.float32)
    return (np.concatenate([cos, cos], 0), np.concatenate([sin, sin], 0))


def _l0_mask():
    BLOCK, LOCAL, VERT = 64, 16, 8
    nb = SEQ // BLOCK
    qb = np.arange(nb)[:, None]
    kb = np.arange(nb)[None, :]
    bm = (kb <= qb) & ((qb - kb < LOCAL) | ((kb + 1) % VERT == 0))
    em = np.kron(bm, np.ones((BLOCK, BLOCK), dtype=bool))
    i = np.arange(SEQ)
    em &= i[:, None] >= i[None, :]
    return em


def _l1_mask():
    i = np.arange(SEQ)
    return (i[None, :] <= i[:, None]) & (i[None, :] > i[:, None] - 1024)


def kernel(L0_attn_hidden_states, L0_attn_w_qkv, L0_attn_b_qkv, L0_attn_w_dense,
           L0_attn_b_dense, L0_ffn_base_W, L0_ffn_gate_W, L0_ffn_lora_A, L0_ffn_lora_B,
           L1_attn_wq, L1_attn_wk, L1_attn_wv, L1_attn_wo,
           L1_ffn_w1, L1_ffn_s1, L1_ffn_w2, L1_ffn_s2, L1_ffn_w3, L1_ffn_s3):
    f32 = np.float32
    xf = np.ascontiguousarray(np.asarray(L0_attn_hidden_states, dtype=f32).reshape(B, SEQ, D))

    bwT = np.ascontiguousarray(np.asarray(L0_ffn_base_W, f32).T)
    gwT = np.ascontiguousarray(np.asarray(L0_ffn_gate_W, f32).T)
    aaT = np.ascontiguousarray(np.asarray(L0_ffn_lora_A, f32).reshape(224, D).T)
    bal = np.ascontiguousarray(np.asarray(L0_ffn_lora_B, f32).transpose(0, 2, 1).reshape(224, D))
    erep = np.zeros((28, 224), f32)
    for e in range(28):
        erep[e, e * 8:(e + 1) * 8] = 1.0

    wqkv = np.asarray(L0_attn_w_qkv, f32).reshape(8, 6, 128, D)
    wqT0 = np.ascontiguousarray(wqkv[:, 0:4].reshape(4096, D).T)
    wkT0 = np.ascontiguousarray(wqkv[:, 4].reshape(1024, D).T)
    wvT0 = np.ascontiguousarray(wqkv[:, 5].reshape(1024, D).T)
    bq = np.asarray(L0_attn_b_qkv, f32).reshape(8, 6, 128)
    bqkv0 = np.zeros((128, 48), f32)
    bqkv0[:, 0:32] = bq[:, 0:4].reshape(32, 128).T
    bqkv0[:, 32:40] = bq[:, 4].reshape(8, 128).T
    bqkv0[:, 40:48] = bq[:, 5].reshape(8, 128).T
    woT0 = np.ascontiguousarray(np.asarray(L0_attn_w_dense, f32).T)
    bo0 = np.asarray(L0_attn_b_dense, f32)

    wqT1 = np.ascontiguousarray(np.asarray(L1_attn_wq, f32))
    wkT1 = np.ascontiguousarray(np.asarray(L1_attn_wk, f32))
    wvT1 = np.ascontiguousarray(np.asarray(L1_attn_wv, f32))
    woT1 = np.ascontiguousarray(np.asarray(L1_attn_wo, f32))
    bqkv1 = np.zeros((128, 48), f32)
    bo1 = np.zeros(4096, f32)

    pk1 = np.ascontiguousarray(np.asarray(L1_ffn_w1, np.int32).reshape(HC1, 128, 2048))
    pk3 = np.ascontiguousarray(np.asarray(L1_ffn_w3, np.int32).reshape(HC1, 128, 2048))
    pk2 = np.ascontiguousarray(np.asarray(L1_ffn_w2, np.int32).reshape(32, 128, 5504))
    s1 = np.ascontiguousarray(np.asarray(L1_ffn_s1, f32).reshape(HC1, 128, 64))
    s3 = np.ascontiguousarray(np.asarray(L1_ffn_s3, f32).reshape(HC1, 128, 64))
    s2 = np.ascontiguousarray(np.asarray(L1_ffn_s2, f32).reshape(32, 128, 172))

    m0 = _l0_mask()
    m1 = _l1_mask()
    pos = np.arange(SEQ)
    maps = []
    perms = []
    for c in range(NCORE):
        b = c // 2
        hb = c % 2
        own = np.arange(1024 * hb, 1024 * (hb + 1))
        other = np.arange(1024 * (1 - hb), 1024 * (2 - hb))
        perm = np.concatenate([own, other])
        perms.append((b, own))
        c0, s0 = _rope_tables(1000000.0, pos[perm])
        c1, s1r_ = _rope_tables(10000.0, pos[perm])
        m0T = np.where(m0[np.ix_(perm, perm)], 0.0, NEG).astype(f32).T
        m1T = np.where(m1[np.ix_(own, perm)], 0.0, NEG).astype(f32).T
        maps.append({
            "x0": np.ascontiguousarray(xf[b][perm]),
            "bwT": bwT, "gwT": gwT, "aaT": aaT, "bal": bal, "erep": erep,
            "wqT0": wqT0, "wkT0": wkT0, "wvT0": wvT0, "woT0": woT0,
            "bqkv0": bqkv0, "bo0": bo0,
            "cos0": np.ascontiguousarray(c0), "sin0": np.ascontiguousarray(s0),
            "mask0T": np.ascontiguousarray(m0T),
            "wqT1": wqT1, "wkT1": wkT1, "wvT1": wvT1, "woT1": woT1,
            "bqkv1": bqkv1, "bo1": bo1,
            "cos1": np.ascontiguousarray(c1), "sin1": np.ascontiguousarray(s1r_),
            "mask1T": np.ascontiguousarray(m1T),
            "pk1": pk1, "s1": s1, "pk3": pk3, "s3": s3, "pk2": pk2, "s2": s2,
        })
    nc = _get("mega", build_mega)
    res = run_bass_kernel_spmd(nc, maps, core_ids=list(range(NCORE)))
    out = np.empty((B, SEQ, D), f32)
    for c in range(NCORE):
        b, own = perms[c]
        out[b][own] = res.results[c]["xout"]
    return out
